# version 14
# baseline (speedup 1.0000x reference)
import numpy as np
import ml_dtypes
from contextlib import ExitStack

import concourse.bass as bass
import concourse.mybir as mybir
from concourse.bass_utils import run_bass_kernel_spmd

F32 = mybir.dt.float32
BF16 = mybir.dt.bfloat16
AF = mybir.ActivationFunctionType
ALU = mybir.AluOpType
AX = mybir.AxisListType

DM, DI, NH, HD, DS = 1024, 2048, 32, 64, 128
CONVD = DI + 2 * DS
DINPROJ = 2 * DI + 2 * DS + NH
NE, DF = 9, 2816
EPS = 1e-5

L, NCH = 256, 16
T = L * NCH
BLK = 4
NBLK = NCH // BLK
BT = L * BLK
SEQ_CHUNKS = 8
KD, KI, KF, CM = DM // 128, DI // 128, DF // 128, CONVD // 128
MASKVAL = 1e9
ACCUM_RESETS = True


class Sem:
    def __init__(self, st, nc, name):
        self.h = st.enter_context(nc.semaphore(name))
        self.n = 0
        self.name = name


class Prog:
    def __init__(self):
        self.ops = {k: [] for k in ("sync", "tensor", "vector", "scalar", "gpsimd")}

    def wait(self, eng, sem, val=None):
        v = sem.n if val is None else val
        if v > 0:
            self.ops[eng].append(("w", sem, v))

    def op(self, eng, fn, incs=()):
        for s, k in incs:
            s.n += k
        self.ops[eng].append(("o", fn, incs))

    def emit(self, block):
        def make(name):
            def body(eng):
                hi = {}
                for item in self.ops[name]:
                    if item[0] == "w":
                        _, sem, v = item
                        if hi.get(sem.name, -1) >= v:
                            continue
                        hi[sem.name] = v
                        eng.wait_ge(sem.h, v)
                    else:
                        _, fn, incs = item
                        inst = fn(eng)
                        for s, k in incs:
                            inst.then_inc(s.h, k)
            return body
        block.sync(make("sync"))
        block.tensor(make("tensor"))
        block.vector(make("vector"))
        block.scalar(make("scalar"))
        block.gpsimd(make("gpsimd"))


def build(n_chunks=NCH, dbg=(), do_moe=True):
    nc = bass.Bass("TRN2", target_bir_lowering=False, debug=False)
    dbg = set(dbg)

    x_h = nc.dram_tensor("x", [T, DM], F32, kind="ExternalInput").ap()
    w_in_h = nc.dram_tensor("w_in", [128, 17 * 2048 + 256], F32, kind="ExternalInput").ap()
    conv_w_h = nc.dram_tensor("conv_w", [CONVD, 4], F32, kind="ExternalInput").ap()
    conv_b_h = nc.dram_tensor("conv_b", [CONVD], F32, kind="ExternalInput").ap()
    dt_bias_h = nc.dram_tensor("dt_bias", [NH], F32, kind="ExternalInput").ap()
    a_log_h = nc.dram_tensor("a_log", [NH], F32, kind="ExternalInput").ap()
    d_h = nc.dram_tensor("dd", [NH], F32, kind="ExternalInput").ap()
    normw_h = nc.dram_tensor("norm_w", [DI], F32, kind="ExternalInput").ap()
    w_out_h = nc.dram_tensor("w_out", [128, 8 * 2048], F32, kind="ExternalInput").ap()
    wr_h = nc.dram_tensor("wr", [DM, NE], F32, kind="ExternalInput").ap()
    wg_h = nc.dram_tensor("wg", [NE, 128, KD * DF], F32, kind="ExternalInput").ap()
    wu_h = nc.dram_tensor("wu", [NE, 128, KD * DF], F32, kind="ExternalInput").ap()
    wd_h = nc.dram_tensor("wd", [NE, 128, KF * DM], F32, kind="ExternalInput").ap()
    idtb_h = nc.dram_tensor("idt_bf", [128, 128], BF16, kind="ExternalInput").ap()
    idtf_h = nc.dram_tensor("idt_f", [128, 128], F32, kind="ExternalInput").ap()
    ones_h = nc.dram_tensor("ones_r", [1, 128], F32, kind="ExternalInput").ap()
    maskbig_h = nc.dram_tensor("maskbig", [2, 128, 512], F32, kind="ExternalInput").ap()

    out_h = nc.dram_tensor("out", [T, DM], F32, kind="ExternalOutput").ap()
    dbg_h = {}
    a_scr_h = nc.dram_tensor("a_scr", [2, 2, NH * L], BF16, kind="Internal").ap()

    st = ExitStack()
    with st:
        SBt = lambda name, shape, dt=F32: st.enter_context(nc.sbuf_tensor(name, shape, dt)).ap()
        PSt = lambda name, shape, dt=F32: st.enter_context(nc.psum_tensor(name, shape, dt)).ap()

        idt_bf = SBt("idt_bf_s", [128, 128], BF16)
        idt_f = SBt("idt_f_s", [128, 128], F32)
        ones_r = SBt("ones_r_s", [1, 128], F32)
        ones_bf = SBt("ones_bf_s", [1, 128], BF16)
        maskdup = SBt("maskdup_s", [128, 2, 512], BF16)
        conv_w = SBt("conv_wc", [128, CM, 4], BF16)
        conv_b = SBt("conv_bc", [128, CM], BF16)
        dt_bias = SBt("dt_biasc", [NH, 1], F32)
        a_neg = SBt("a_neg", [NH, 1], F32)
        d_bc = SBt("d_bc", [128, NH], F32)
        normw_bc = SBt("normw_bc", [128, DI], BF16)
        wr_bf = SBt("wr_bf", [128, KD, NE], BF16)
        stat = SBt("stat_c", [128, 16], F32)
        s_f32 = SBt("s_f32", [128, NH * HD], F32)
        s_bfb = SBt("s_bfb", [128, NH * HD], BF16)
        halo = SBt("halo", [128, CM, 3], BF16)
        x2_blk = SBt("x2_blk", [128, 2 * BLK, DM], F32)
        x2_fm = SBt("x2_fm", [128, KD, BT], BF16)
        cw_blk = SBt("cw_blk", [128, 2 * BLK, 16], F32)
        probs_p = SBt("probs_p", [128, 16], F32)
        m8_p = SBt("m8_p", [128, 8], F32)
        zap_p = SBt("zap_p", [128, 16], F32)
        al_row = SBt("al_row", [1, NH], F32)
        b_tm = SBt("b_tm", [128, 2, 128], BF16)

        AR = 141632
        arena = st.enter_context(nc.sbuf_tensor("arena", [128, AR], mybir.dt.uint8)).ap()

        class Alloc:
            def __init__(self):
                self.off = 0

            def take(self, nbytes, align=64):
                o = (self.off + align - 1) // align * align
                self.off = o + nbytes
                assert self.off <= AR, f"arena overflow {self.off}"
                return o

        def view(off, shape, dt, pdim=128):
            isz = mybir.dt.size(dt)
            n = 1
            for s in shape[1:]:
                n *= s
            ap = arena[0:pdim, off:off + n * isz].bitcast(dt)
            if len(shape) == 2:
                return ap
            if len(shape) == 3:
                return ap.rearrange("p (a b) -> p a b", a=shape[1])
            raise ValueError(shape)

        A = Alloc()
        x_tm = [view(A.take(2 * DM * 4), [128, 2, DM], F32) for _ in range(2)]
        xbf = view(A.take(2 * DM * 2), [128, 2, DM], BF16)
        x_fm = view(A.take(KD * L * 2), [128, KD, L], BF16)
        z_silu = view(A.take(2 * DI * 2), [128, 2, DI], BF16)
        conv_in = view(A.take(CM * (3 + L) * 2), [128, CM, 3 + L], BF16)
        o_ct = A.take(max(2 * 9 * L * 2, DI * 4))
        conv_t1 = view(o_ct, [128, 9, L], BF16)
        conv_t2 = view(o_ct + 9 * L * 2, [128, 9, L], BF16)
        tmp2k = view(o_ct, [128, DI], F32)
        sq_scr = view(o_ct, [128, DM], F32)
        conv_s = view(A.take(CM * L * 2), [128, CM, L], BF16)
        dt_raw = view(A.take(L * 4), [NH, L], F32, pdim=NH)
        af_bf = [view(A.take(1024 * 2), [1, 1024], BF16, pdim=1) for _ in range(3)]
        a_hi_bf = view(A.take(L * 2), [NH, L], BF16, pdim=NH)
        a_lo_bf = view(A.take(L * 2), [NH, L], BF16, pdim=NH)
        dt_w0 = view(A.take(L * 4), [NH, L], F32, pdim=NH)
        dt_w1 = view(A.take(L * 4), [NH, L], F32, pdim=NH)
        a_fm = view(A.take(L * 4), [NH, L], F32, pdim=NH)
        ds_fm = view(A.take(L * 4), [NH, L], F32, pdim=NH)
        a_tm = view(A.take(2 * NH * 4), [128, 2, NH], F32)
        a_tm_neg = view(A.take(2 * NH * 4), [128, 2, NH], F32)
        ea_tm = view(A.take(2 * NH * 4), [128, 2, NH], F32)
        ds_tm = view(A.take(2 * NH * 4), [128, 2, NH], F32)
        dt_tm = view(A.take(2 * NH * 4), [128, 2, NH], F32)
        cd_bc = view(A.take(NH * 4), [128, NH], F32)
        o_xs = A.take(2 * DI * 2)
        xssm_tm = view(o_xs, [128, 2, DI], BF16)
        y_fm = view(o_xs, [128, KI, L], BF16)
        o_xt = A.take(2 * DI * 2)
        xt_tm = view(o_xt, [128, 2, DI], BF16)
        yn_bf = view(o_xt, [128, 2, DI], BF16)
        o_xd = A.take(2 * DI * 2)
        xtds_tm = view(o_xd, [128, 2, DI], BF16)
        x2bf = view(o_xd, [128, 2, DM], BF16)
        cbt_bf = view(A.take(2 * L * 2), [128, 2, L], BF16)
        s_t = [view(A.take(2 * L * 2), [128, 2, L], BF16) for _ in range(2)]
        y_sb = view(A.take(2 * DI * 4), [128, 2, DI], F32)
        o_wi = [A.take(KD * 256 * 2) for _ in range(3)]
        win_ring = [view(o, [128, KD, 256], BF16) for o in o_wi]
        win_ring32 = [view(o, [128, KD, 32], BF16) for o in o_wi]
        win_flat = [view(o, [128, KD * 256], BF16) for o in o_wi]
        o_wo = [A.take(KI * 128 * 2) for _ in range(2)]
        wout_ring = [view(o, [128, KI, 128], BF16) for o in o_wo]
        wout_flat = [view(o, [128, KI * 128], BF16) for o in o_wo]
        stag = view(0, [1, DI], F32, pdim=1)
        mamba_bytes = A.off

        A = Alloc()
        o_wg = [A.take(KD * 256 * 2) for _ in range(3)]
        wg_ring = [view(o, [128, KD, 256], BF16) for o in o_wg]
        wg_flat = [view(o, [128, KD * 256], BF16) for o in o_wg]
        o_wu = [A.take(KD * 256 * 2) for _ in range(3)]
        wu_ring = [view(o, [128, KD, 256], BF16) for o in o_wu]
        wu_flat = [view(o, [128, KD * 256], BF16) for o in o_wu]
        o_wd = [A.take(KF * DM * 2) for _ in range(2)]
        wd_buf = [view(o, [128, KF, DM], BF16) for o in o_wd]
        wd_flat = [view(o, [128, KF * DM], BF16) for o in o_wd]
        o_hid = A.take(KF * 512 * 2)
        hid_bf = view(o_hid, [128, KF, 512], BF16)
        dump_m = view(o_hid, [128, DM], F32)
        gate_s = [view(A.take(512 * 2), [128, 512], BF16) for _ in range(2)]
        moe_bytes = A.off
        assert max(mamba_bytes, moe_bytes) <= AR, (mamba_bytes, moe_bytes)

        pb = [PSt(f"pb{i}", [128, 512], F32) for i in range(7)]
        ptr = PSt("ptr", [128, 1024], BF16)

        s_sp = Sem(st, nc, "s_sp")
        s_pe = Sem(st, nc, "s_pe")
        s_dv = Sem(st, nc, "s_dv")
        s_ac = Sem(st, nc, "s_ac")
        s_xb = Sem(st, nc, "s_xb")
        s_xt = Sem(st, nc, "s_xt")
        s_as = Sem(st, nc, "s_as")
        s_af = [Sem(st, nc, f"s_af{i}") for i in range(3)]
        s_stt = [Sem(st, nc, "s_st0"), Sem(st, nc, "s_st1")]
        s_wi = [Sem(st, nc, f"s_wi{i}") for i in range(3)]
        s_wo = [Sem(st, nc, f"s_wo{i}") for i in range(2)]
        s_wg = [Sem(st, nc, f"s_wg{i}") for i in range(3)]
        s_wu = [Sem(st, nc, f"s_wu{i}") for i in range(3)]
        s_wd = [Sem(st, nc, f"s_wd{i}") for i in range(2)]
        s_in = Sem(st, nc, "s_in")
        s_tap = Sem(st, nc, "s_tap")

        P = Prog()
        bank_rel = [None] * 8

        def dma(q, out, in_, sem, deps=(), slow=False):
            for dsem, dval in deps:
                P.wait(q, dsem, dval)
            P.op(q, lambda e, o=out, i=in_, s=slow: e.dma_start(
                out=o, in_=i, allow_slow_non_contiguous=s), [(sem, 16)])

        def pe(fn, inc=False, deps=()):
            for dsem, dval in deps:
                P.wait("tensor", dsem, dval)
            P.op("tensor", fn, [(s_pe, 1)] if inc else ())

        def dv(fn, inc=True, deps=()):
            P.wait("vector", s_dv)
            for dsem, dval in deps:
                P.wait("vector", dsem, dval)
            P.op("vector", fn, [(s_dv, 1)] if inc else ())

        def ac(fn, inc=True, deps=()):
            P.wait("scalar", s_ac)
            for dsem, dval in deps:
                P.wait("scalar", dsem, dval)
            P.op("scalar", fn, [(s_ac, 1)] if inc else ())

        def D(sem):
            return (sem, sem.n)

        def fence():
            return [D(s_pe), D(s_dv), D(s_ac)]

        def brel(i):
            return [bank_rel[i]] if bank_rel[i] else []

        def dbg_tap(name, ap, shape, dt=F32):
            if name not in dbg:
                return
            h = nc.dram_tensor("dbg_" + name, list(shape), dt, kind="ExternalOutput").ap()
            dbg_h[name] = h
            prev = s_tap.n
            dma("sync", h, ap, s_tap, deps=fence() + [(s_tap, prev)])
            for eng in ("vector", "scalar", "tensor", "gpsimd"):
                P.wait(eng, s_tap, s_tap.n)

        def accum_stat(col, in_ap, func, dump, deps=()):
            if ACCUM_RESETS:
                dv(lambda e, col=col: e.memset(stat[:, col:col + 1], 0.0),
                   deps=list(deps))
                deps = [D(s_dv)]
            ac(lambda e, col=col, in_ap=in_ap, func=func, dump=dump: e.activation(
                dump, in_ap, func, accum_out=stat[:, col:col + 1]), deps=list(deps))

        dma("sync", idt_bf, idtb_h, s_sp)
        dma("sync", idt_f, idtf_h, s_sp)
        dma("sync", ones_r, ones_h, s_sp)
        dma("gpsimd", maskdup, maskbig_h.rearrange("j p n -> p j n"), s_in)
        dma("sync", dt_bias, dt_bias_h.unsqueeze(1), s_sp)
        dma("sync", a_neg, a_log_h.unsqueeze(1), s_sp)
        dma("sync", stag, normw_h.unsqueeze(0), s_sp)
        ld1 = s_sp.n
        dma("gpsimd", conv_w, conv_w_h.rearrange("(m p) k -> p m k", p=128), s_in)
        dma("gpsimd", conv_b, conv_b_h.rearrange("(m p) -> p m", p=128), s_in, slow=True)
        dma("gpsimd", wr_bf, wr_h.rearrange("(k p) e -> p k e", p=128), s_in)
        init_in = s_in.n

        ac(lambda e: e.activation(a_neg, a_neg, AF.Exp), deps=[(s_sp, ld1)])
        dv(lambda e: e.tensor_scalar_mul(a_neg, a_neg, -1.0), deps=[D(s_ac)])
        dv(lambda e: e.tensor_copy(ones_bf, ones_r), deps=[(s_sp, ld1)])
        for i in range(4):
            pe(lambda e, i=i: e.matmul(pb[0][:, 0:512], ones_r,
                                       stag[0:1, 512 * i:512 * (i + 1)], start=True, stop=True),
               inc=True, deps=[(s_sp, ld1)] + ([D(s_dv)] if i else []))
            dv(lambda e, i=i: e.tensor_copy(normw_bc[:, 512 * i:512 * (i + 1)], pb[0][:, 0:512]),
               deps=[D(s_pe)])
        dma("sync", stag[0:1, 0:NH], d_h.unsqueeze(0), s_sp, deps=[D(s_pe)])
        pe(lambda e: e.matmul(pb[0][:, 0:NH], ones_r, stag[0:1, 0:NH], start=True, stop=True),
           inc=True, deps=[D(s_sp), D(s_dv)])
        dv(lambda e: e.tensor_copy(d_bc, pb[0][:, 0:NH]), deps=[D(s_pe)])
        dv(lambda e: e.memset(s_f32, 0.0))
        dv(lambda e: e.memset(x2_blk, 0.0))
        dv(lambda e: e.memset(cw_blk, 0.0))
        dv(lambda e: e.memset(s_bfb, 0.0))
        dv(lambda e: e.memset(halo, 0.0))
        dv(lambda e: e.memset(probs_p, -1.0))
        dv(lambda e: e.memset(stat[:, 15:16], EPS))
        dv(lambda e: e.memset(stat[:, 14:15], 1.0))
        bank_rel[0] = D(s_dv)

        WIN_NS, WOUT_NS = 18, 8
        win_rel = [None, None, None]
        wout_rel = [None, None]
        win_tag = {}
        wout_tag = {}

        def load_win(c, sl, extra=()):
            slot = (c * WIN_NS + sl) % 3
            wid = 32 if sl == 17 else 256
            deps = list(extra) + ([win_rel[slot]] if win_rel[slot] else [])
            dma("gpsimd", win_flat[slot][:, 0:KD * wid],
                w_in_h[:, sl * 2048:sl * 2048 + KD * wid], s_wi[slot], deps=deps)
            win_tag[(c, sl)] = (slot, s_wi[slot].n)

        def load_wout(c, sl, extra=()):
            slot = (c * WOUT_NS + sl) % 2
            deps = list(extra) + ([wout_rel[slot]] if wout_rel[slot] else [])
            dma("gpsimd", wout_flat[slot], w_out_h[:, sl * 2048:(sl + 1) * 2048],
                s_wo[slot], deps=deps)
            wout_tag[(c, sl)] = (slot, s_wo[slot].n)

        def transpose8(srcs, dst_ap, first_deps):
            deps = list(first_deps) + brel(7)
            for i, s in enumerate(srcs):
                pe(lambda e, i=i, s=s: e.transpose(ptr[:, 128 * i:128 * (i + 1)], s, idt_bf),
                   inc=(i == 7), deps=(deps if i == 0 else []))
            ac(lambda e, d=dst_ap: e.activation(
                d, ptr.rearrange("p (a m t) -> p a m t", a=4, m=2), AF.Copy),
               deps=[D(s_pe)])
            bank_rel[7] = D(s_ac)

        prev_scan = {"pe": 0, "dv": 0}
        SLAB_ORDER = list(range(8, 17)) + [17] + list(range(8))

        cst = {}
        st_rel = [None, None]

        def a_dma(c, phase_deps=()):
            xs = c % 2
            xr = x_h[c * L:(c + 1) * L, :].rearrange("(m p) n -> p m n", p=128)
            xdeps = list(phase_deps)
            if c - 2 in cst:
                xdeps.append(cst[c - 2]["r1_dv"])
            dma("sync", x_tm[xs], xr, s_xt, deps=xdeps)
            bdeps = list(phase_deps)
            if c - 1 in cst and "xfm_pe" in cst[c - 1]:
                bdeps.append(cst[c - 1]["xfm_pe"])
            dma("gpsimd", xbf, xr, s_xb, deps=bdeps)
            cst[c] = {"x_ld": s_xt.n, "xb_ld": s_xb.n}
            for j in range(3):
                load_win(c, SLAB_ORDER[j])

        def a_xfm(c):
            S = cst[c]
            seq_start = (c % SEQ_CHUNKS) == 0
            tdeps = [(s_xb, S["xb_ld"])]
            for g in range(2):
                transpose8(
                    [xbf[:, i % 2, 128 * (g * 4 + i // 2):128 * (g * 4 + i // 2 + 1)]
                     for i in range(8)],
                    x_fm[:, 4 * g:4 * (g + 1), :].rearrange("p a (m t) -> p a m t", m=2),
                    (tdeps if g == 0 else []))
            S["xfm_dv"] = s_ac.n
            S["xfm_pe"] = D(s_pe)
            if seq_start:
                dv(lambda e: e.memset(conv_in[:, :, 0:3], 0.0))
            else:
                dv(lambda e: e.tensor_copy(conv_in[:, :, 0:3], halo))

        def a_inproj_slab(c, sl):
            S = cst[c]
            slot, fill = win_tag[(c, sl)]
            bi = sl % 2
            bank = pb[bi]
            dep0 = [(s_wi[slot], fill), (s_ac, S["xfm_dv"])] + brel(bi)
            if sl < 8:
                for mt in range(2):
                    for kt in range(KD):
                        pe(lambda e, mt=mt, kt=kt, slot=slot, bank=bank: e.matmul(
                            bank[:, 256 * mt:256 * (mt + 1)],
                            x_fm[:, kt, 128 * mt:128 * (mt + 1)],
                            win_ring[slot][:, kt, 0:256],
                            start=(kt == 0), stop=(kt == KD - 1)),
                           inc=(mt == 1 and kt == KD - 1),
                           deps=(dep0 if (mt == 0 and kt == 0) else []))
                win_rel[slot] = D(s_pe)
                j = SLAB_ORDER.index(sl)
                if j + 3 < WIN_NS:
                    load_win(c, SLAB_ORDER[j + 3])
                if j == WIN_NS - 1:
                    S["ip_pe"] = D(s_pe)
                acdeps = [D(s_pe)]
                if c - 1 in cst and "gate_dv" in cst[c - 1]:
                    acdeps.append((s_dv, cst[c - 1]["gate_dv"]))
                ac(lambda e, sl=sl, bank=bank: e.activation(
                    z_silu[:, :, 256 * sl:256 * (sl + 1)],
                    bank.rearrange("p (m t) -> p m t", m=2), AF.Silu),
                   deps=acdeps)
                bank_rel[bi] = D(s_ac)
            elif sl < 17:
                m0 = 2 * (sl - 8)
                for mi in range(2):
                    for kt in range(KD):
                        pe(lambda e, mi=mi, kt=kt, slot=slot, bank=bank: e.matmul(
                            bank[:, 256 * mi:256 * (mi + 1)],
                            win_ring[slot][:, kt, 128 * mi:128 * (mi + 1)],
                            x_fm[:, kt, :],
                            start=(kt == 0), stop=(kt == KD - 1)),
                           inc=(mi == 1 and kt == KD - 1),
                           deps=(dep0 if (mi == 0 and kt == 0) else []))
                win_rel[slot] = D(s_pe)
                j = SLAB_ORDER.index(sl)
                if j + 3 < WIN_NS:
                    load_win(c, SLAB_ORDER[j + 3])
                dv(lambda e, m0=m0, bank=bank: e.tensor_copy(
                    conv_in[:, m0:m0 + 2, 3:3 + L],
                    bank.rearrange("p (m t) -> p m t", m=2)),
                   deps=[D(s_pe)])
                bank_rel[bi] = D(s_dv)
                if sl == 16:
                    S["convin_dv"] = s_dv.n
            else:
                for kt in range(KD):
                    pe(lambda e, kt=kt, slot=slot, bank=bank: e.matmul(
                        bank[0:NH, 0:L], win_ring32[slot][:, kt, 0:NH], x_fm[:, kt, :],
                        start=(kt == 0), stop=(kt == KD - 1)),
                       inc=(kt == KD - 1), deps=(dep0 if kt == 0 else []))
                win_rel[slot] = D(s_pe)
                j = SLAB_ORDER.index(sl)
                if j + 3 < WIN_NS:
                    load_win(c, SLAB_ORDER[j + 3])
                dv(lambda e, bank=bank: e.tensor_copy(dt_raw, bank[0:NH, 0:L]),
                   deps=[D(s_pe)])
                bank_rel[bi] = D(s_dv)
                S["dtraw_dv"] = s_dv.n

        def a_conv(c):
            S = cst[c]
            for h2 in range(2):
                cs = slice(9 * h2, 9 * h2 + 9)
                dv(lambda e, cs=cs: e.tensor_tensor(
                    conv_t1, conv_in[:, cs, 0:L],
                    conv_w[:, cs, 0:1].broadcast_to([128, 9, L]), ALU.mult),
                   deps=([(s_dv, S["convin_dv"]), (s_in, init_in)] if h2 == 0
                         else [D(s_ac)]))
                for k in range(1, 4):
                    dv(lambda e, cs=cs, k=k: e.tensor_tensor(
                        conv_t2, conv_in[:, cs, k:k + L],
                        conv_w[:, cs, k:k + 1].broadcast_to([128, 9, L]), ALU.mult))
                    dv(lambda e: e.tensor_tensor(conv_t1, conv_t1, conv_t2, ALU.add))
                dv(lambda e, cs=cs: e.tensor_tensor(
                    conv_t1, conv_t1,
                    conv_b[:, cs].unsqueeze(2).broadcast_to([128, 9, L]), ALU.add))
                acdeps = [D(s_dv)]
                if h2 == 0 and c - 1 in cst and "cs_free" in cst[c - 1]:
                    acdeps.append(cst[c - 1]["cs_free"])
                ac(lambda e, cs=cs: e.activation(conv_s[:, cs, :], conv_t1, AF.Silu),
                   deps=acdeps)
            S["conv_ac"] = s_ac.n
            dbg_tap("conv%d" % c, conv_s, [128, CM, L], BF16)
            dbg_tap("zs%d" % c, z_silu, [128, 2, DI], BF16)
            dv(lambda e: e.tensor_copy(halo, conv_in[:, :, L:L + 3]))

        def a_dt(c):
            S = cst[c]
            dv(lambda e: e.tensor_scalar(dt_w0, dt_raw, dt_bias, None, ALU.add),
               deps=[(s_dv, S["dtraw_dv"])])
            dv(lambda e: e.scalar_tensor_tensor(dt_w1, dt_w0, -1.0, dt_w0, ALU.mult, ALU.max),
               deps=[D(s_dv)])
            ac(lambda e: e.activation(dt_w1, dt_w1, AF.Exp, scale=-1.0), deps=[D(s_dv)])
            ac(lambda e: e.activation(dt_w1, dt_w1, AF.Ln, bias=stat[0:NH, 14:15]))
            dv(lambda e: e.tensor_scalar(dt_w0, dt_w0, 0.0, None, ALU.max), deps=[D(s_dv)])
            dv(lambda e: e.tensor_tensor(dt_w0, dt_w0, dt_w1, ALU.add),
               deps=[D(s_ac), D(s_dv)])
            dv(lambda e: e.tensor_scalar(dt_w1, dt_w0, a_neg, None, ALU.mult),
               deps=[D(s_dv)])
            dv(lambda e: e.tensor_tensor_scan(a_fm, dt_w1, dt_w1, 0.0, ALU.add, ALU.bypass),
               deps=[D(s_dv)])
            dv(lambda e: e.tensor_scalar(ds_fm, a_fm, a_fm[:, L - 1:L], None, ALU.subtract),
               deps=[D(s_dv)])
            ac(lambda e: e.activation(ds_fm, ds_fm, AF.Exp, scale=-1.0), deps=[D(s_dv)])

            dv(lambda e: e.tensor_copy(a_hi_bf, a_fm), deps=[D(s_ac)])
            dv(lambda e: e.tensor_copy(dt_w1, a_hi_bf))
            dv(lambda e: e.tensor_tensor(dt_w1, a_fm, dt_w1, ALU.subtract))
            dv(lambda e: e.tensor_copy(a_lo_bf, dt_w1))
            asl = a_scr_h[c % 2]
            dma("sync", asl[0].rearrange("(h l) -> h l", h=NH), a_hi_bf, s_as,
                deps=[D(s_dv), D(s_pe)])
            dma("sync", asl[1].rearrange("(h l) -> h l", h=NH), a_lo_bf, s_as)
            S["ascr_ld"] = s_as.n
            S["dtpipe_dv"] = s_dv.n
            S["dtpipe_ac"] = s_ac.n

        def a_dt_tr(c):
            S = cst[c]
            tr_deps = [(s_ac, S["dtpipe_ac"]), (s_dv, S["dtpipe_dv"])] + brel(6)
            for idx, src in enumerate((dt_w0, a_fm, ds_fm)):
                for jt in range(2):
                    pe(lambda e, src=src, jt=jt, idx=idx: e.transpose(
                        pb[6][:, 64 * idx + 32 * jt:64 * idx + 32 * (jt + 1)],
                        src[:, 128 * jt:128 * (jt + 1)], idt_f[0:NH, 0:NH]),
                       inc=(idx == 2 and jt == 1),
                       deps=(tr_deps if (idx == 0 and jt == 0) else []))
            for idx, dst in enumerate((dt_tm, a_tm, ds_tm)):
                dv(lambda e, idx=idx, dst=dst: e.tensor_copy(
                    dst, pb[6][:, 64 * idx:64 * (idx + 1)].rearrange("p (j h) -> p j h", j=2)),
                   deps=[D(s_pe)])
            bank_rel[6] = D(s_dv)
            dv(lambda e: e.tensor_scalar_mul(a_tm_neg, a_tm, -1.0), deps=[D(s_ac)])
            ac(lambda e: e.activation(ea_tm, a_tm, AF.Exp), deps=[D(s_dv)])
            pe(lambda e: e.transpose(pb[6][0:1, 0:NH], a_fm[:, L - 1:L],
                                     idt_f[0:NH, 0:NH]),
               inc=True, deps=brel(6))
            dv(lambda e: e.tensor_copy(al_row[0:1, 0:NH], pb[6][0:1, 0:NH]),
               deps=[D(s_pe)])
            pe(lambda e: e.matmul(pb[6][:, 0:NH], ones_r, al_row, start=True, stop=True),
               inc=True, deps=[D(s_dv)])
            ac(lambda e: e.activation(cd_bc, pb[6][:, 0:NH], AF.Exp), deps=[D(s_pe), D(s_dv)])
            bank_rel[6] = D(s_ac)

        def b_head(c):
            S = cst[c]
            seq_start = (c % SEQ_CHUNKS) == 0
            conv_ac = S["conv_ac"]
            for sl in range(2):
                load_wout(c, sl)
            for b4 in range(4):
                transpose8(
                    [conv_s[:, 4 * b4 + i // 2, 128 * (i % 2):128 * (i % 2 + 1)]
                     for i in range(8)],
                    xssm_tm[:, :, 512 * b4:512 * (b4 + 1)]
                        .rearrange("p m (a t) -> p a m t", a=4),
                    [(s_ac, conv_ac)])
            a_dt_tr(c)
            bdeps = [(s_ac, conv_ac)] + brel(7)
            for jt in range(2):
                pe(lambda e, jt=jt: e.transpose(ptr[:, 128 * jt:128 * (jt + 1)],
                                                conv_s[:, 16, 128 * jt:128 * (jt + 1)], idt_bf),
                   inc=(jt == 1), deps=(bdeps if jt == 0 else []))
            dv(lambda e: e.tensor_copy(b_tm, ptr[:, 0:256].rearrange("p (j n) -> p j n", j=2)),
               deps=[D(s_pe)])
            bank_rel[7] = D(s_dv)

            for jt in range(2):
                dv(lambda e, jt=jt: e.tensor_tensor(
                    xt_tm[:, jt, :].rearrange("p (h q) -> p h q", h=NH),
                    xssm_tm[:, jt, :].rearrange("p (h q) -> p h q", h=NH),
                    dt_tm[:, jt, :].unsqueeze(2).broadcast_to([128, NH, HD]), ALU.mult),
                   deps=([D(s_pe), D(s_ac)] if jt == 0 else []))
            for jt in range(2):
                dv(lambda e, jt=jt: e.tensor_tensor(
                    xtds_tm[:, jt, :].rearrange("p (h q) -> p h q", h=NH),
                    xt_tm[:, jt, :].rearrange("p (h q) -> p h q", h=NH),
                    ds_tm[:, jt, :].unsqueeze(2).broadcast_to([128, NH, HD]), ALU.mult))
            xt_dv = s_dv.n

            if not seq_start:
                for it in range(2):
                    for ncq in range(4):
                        bi = 2 + (it * 4 + ncq) % 2
                        pe(lambda e, it=it, ncq=ncq, bi=bi: e.matmul(
                            pb[bi], conv_s[:, 17, 128 * it:128 * (it + 1)],
                            s_bfb[:, 512 * ncq:512 * (ncq + 1)], start=True, stop=True),
                           inc=True,
                           deps=([(s_ac, conv_ac), (s_dv, prev_scan["dv"])] + brel(bi)))
                        dv(lambda e, it=it, ncq=ncq, bi=bi: e.tensor_tensor(
                            y_sb[:, it, 512 * ncq:512 * (ncq + 1)]
                                .rearrange("p (h q) -> p h q", h=8),
                            pb[bi].rearrange("p (h q) -> p h q", h=8),
                            ea_tm[:, it, 8 * ncq:8 * (ncq + 1)].unsqueeze(2)
                                .broadcast_to([128, 8, HD]), ALU.mult),
                           deps=[D(s_pe), D(s_ac)])
                        bank_rel[bi] = D(s_dv)
                yoff_pe = s_pe.n
                for it in range(2):
                    dv(lambda e, it=it: e.tensor_tensor(
                        tmp2k.rearrange("p (h q) -> p h q", h=NH),
                        xssm_tm[:, it, :].rearrange("p (h q) -> p h q", h=NH),
                        d_bc.unsqueeze(2).broadcast_to([128, NH, HD]), ALU.mult),
                       deps=([D(s_ac)] if it == 0 else []))
                    dv(lambda e, it=it: e.tensor_tensor(
                        y_sb[:, it, :], y_sb[:, it, :], tmp2k, ALU.add))
            else:
                yoff_pe = s_pe.n
                for it in range(2):
                    dv(lambda e, it=it: e.tensor_tensor(
                        y_sb[:, it, :].rearrange("p (h q) -> p h q", h=NH),
                        xssm_tm[:, it, :].rearrange("p (h q) -> p h q", h=NH),
                        d_bc.unsqueeze(2).broadcast_to([128, NH, HD]), ALU.mult),
                       deps=([D(s_ac)] if it == 0 else []))
            S["yinit_dv"] = s_dv.n

            st_banks = []
            for ncq in range(4):
                bi = ncq
                for jt in range(2):
                    pe(lambda e, ncq=ncq, jt=jt, bi=bi: e.matmul(
                        pb[bi], b_tm[:, jt, :], xtds_tm[:, jt, 512 * ncq:512 * (ncq + 1)],
                        start=(jt == 0), stop=(jt == 1)),
                       inc=(jt == 1),
                       deps=(([(s_dv, xt_dv)] + brel(bi)) if jt == 0 else []))
                st_banks.append((bi, s_pe.n))
            if not seq_start:
                dv(lambda e: e.tensor_tensor(
                    s_f32.rearrange("p (h q) -> p h q", h=NH),
                    s_f32.rearrange("p (h q) -> p h q", h=NH),
                    cd_bc.unsqueeze(2).broadcast_to([128, NH, HD]), ALU.mult),
                   deps=[D(s_ac), (s_pe, yoff_pe)])
                for ncq in range(4):
                    bi, pen = st_banks[ncq]
                    dv(lambda e, ncq=ncq, bi=bi: e.tensor_tensor(
                        s_f32[:, 512 * ncq:512 * (ncq + 1)],
                        s_f32[:, 512 * ncq:512 * (ncq + 1)], pb[bi], ALU.add),
                       deps=[(s_pe, pen)])
                    bank_rel[bi] = D(s_dv)
            else:
                for ncq in range(4):
                    bi, pen = st_banks[ncq]
                    dv(lambda e, ncq=ncq, bi=bi: e.tensor_copy(
                        s_f32[:, 512 * ncq:512 * (ncq + 1)], pb[bi]),
                       deps=[(s_pe, pen)])
                    bank_rel[bi] = D(s_dv)
            dv(lambda e: e.tensor_copy(s_bfb, s_f32), deps=[(s_pe, yoff_pe)])
            prev_scan["dv"] = s_dv.n
            dbg_tap("state%d" % c, s_f32, [128, NH * HD])

            for jt in range(2):
                pe(lambda e, jt=jt: e.matmul(
                    pb[6][:, 256 * jt:256 * (jt + 1)],
                    conv_s[:, 16, 128 * jt:128 * (jt + 1)], conv_s[:, 17, :],
                    start=True, stop=True),
                   inc=(jt == 1), deps=(([(s_ac, conv_ac)] + brel(6)) if jt == 0 else []))
            dv(lambda e: e.tensor_copy(cbt_bf, pb[6].rearrange("p (j l) -> p j l", j=2)),
               deps=[D(s_pe)])
            bank_rel[6] = D(s_dv)
            S["cbt_dv"] = s_dv.n
            S["cs_free"] = D(s_pe)

        def b_sloop(c):
            S = cst[c]
            ascr_ld = S["ascr_ld"]
            cbt_dv = S["cbt_dv"]
            yinit_dv = S["yinit_dv"]
            af_fill = {}
            af_rel = b_sloop.af_rel

            def load_aflat(pair):
                slot = pair % 3
                deps = [(s_as, ascr_ld)] + ([af_rel[slot]] if af_rel[slot] else [])
                dma("sync", af_bf[slot].rearrange("o (s n) -> o s n", s=2),
                    a_scr_h[c % 2][:, 512 * pair:512 * (pair + 1)].unsqueeze(0),
                    s_af[slot], deps=deps)
                af_fill[pair] = (slot, s_af[slot].n)

            for _p in range(3):
                load_aflat(_p)

            def ydbank(it, grp):
                return (2 + it) if grp % 2 == 0 else (0 + it)

            for pair in range(NH // 2):
                aslot = pair % 3
                afs, afv = af_fill[pair]
                for jt in range(2):
                    bank = 4 + jt
                    bdep = [(s_af[afs], afv)] + brel(bank)
                    if pair == 0:
                        bdep += [(s_in, init_in)]
                    pe(lambda e, aslot=aslot, bank=bank: e.matmul(
                        pb[bank], ones_bf, af_bf[aslot][0:1, 0:512],
                        start=True, stop=False),
                       deps=bdep)
                    pe(lambda e, aslot=aslot, bank=bank: e.matmul(
                        pb[bank], ones_bf, af_bf[aslot][0:1, 512:1024],
                        start=False, stop=False))
                    pe(lambda e, jt=jt, bank=bank: e.matmul(
                        pb[bank], idt_bf, maskdup[:, jt, :],
                        start=False, stop=True), inc=True)
                ai_pe = s_pe.n
                for hh in range(2):
                    h = 2 * pair + hh
                    sbuf = s_t[h % 2]
                    for jt in range(2):
                        acdeps = [(s_pe, ai_pe)]
                        if jt == 0 and st_rel[h % 2]:
                            acdeps.append(st_rel[h % 2])
                        ac(lambda e, hh=hh, jt=jt, h=h, sbuf=sbuf: e.activation(
                            sbuf[:, jt, :], pb[4 + jt][:, 256 * hh:256 * (hh + 1)],
                            AF.Exp, bias=a_tm_neg[:, jt, h:h + 1]),
                           deps=acdeps)
                        dv(lambda e, jt=jt, sbuf=sbuf: e.tensor_tensor(
                            sbuf[:, jt, :], sbuf[:, jt, :], cbt_bf[:, jt, :], ALU.mult),
                           deps=[D(s_ac), (s_dv, cbt_dv)])
                    s_dv_h = s_dv.n
                    grp = h // 8
                    for it in range(2):
                        bi = ydbank(it, grp)
                        col = (h % 8) * HD
                        first = (h % 8 == 0)
                        for jt in range(2):
                            pe(lambda e, it=it, jt=jt, h=h, bi=bi, col=col, sbuf=sbuf: e.matmul(
                                pb[bi][:, col:col + HD],
                                sbuf[:, jt, 128 * it:128 * (it + 1)],
                                xt_tm[:, jt, h * HD:(h + 1) * HD],
                                start=(jt == 0), stop=(jt == 1)),
                               inc=(jt == 1),
                               deps=(([(s_dv, s_dv_h)] + (brel(bi) if first else []))
                                     if jt == 0 else []))
                    st_rel[h % 2] = D(s_pe)
                bank_rel[4] = D(s_ac)
                bank_rel[5] = D(s_ac)
                if pair + 3 < NH // 2:
                    af_rel[aslot] = D(s_pe)
                    load_aflat(pair + 3)
                if pair % 4 == 3:
                    grp = pair // 4
                    for it in range(2):
                        bi = ydbank(it, grp)
                        dv(lambda e, it=it, grp=grp, bi=bi: e.scalar_tensor_tensor(
                            y_sb[:, it, 512 * grp:512 * (grp + 1)], pb[bi], 1.0,
                            y_sb[:, it, 512 * grp:512 * (grp + 1)], ALU.mult, ALU.add),
                           deps=[D(s_pe), (s_dv, yinit_dv)])
                        bank_rel[bi] = D(s_dv)
            dbg_tap("yssd%d" % c, y_sb, [128, 2, DI])
        b_sloop.af_rel = [None, None, None]

        def b_gate(c):
            S = cst[c]
            for it in range(2):
                dv(lambda e, it=it: e.tensor_tensor(
                    y_sb[:, it, :], y_sb[:, it, :], z_silu[:, it, :], ALU.mult),
                   deps=[D(s_ac)])
                accum_stat(8 + it, y_sb[:, it, :], AF.Square, tmp2k, deps=[D(s_dv)])
                ac(lambda e, it=it: e.activation(stat[:, 8 + it:9 + it],
                                                 stat[:, 8 + it:9 + it], AF.Ln,
                                                 scale=1.0 / DI, bias=stat[:, 15:16]))
                ac(lambda e, it=it: e.activation(stat[:, 8 + it:9 + it],
                                                 stat[:, 8 + it:9 + it], AF.Exp, scale=-0.5))
                dv(lambda e, it=it: e.tensor_scalar(tmp2k, y_sb[:, it, :],
                                                    stat[:, 8 + it:9 + it], None, ALU.mult),
                   deps=[D(s_ac)])
                dv(lambda e, it=it: e.tensor_tensor(yn_bf[:, it, :], tmp2k, normw_bc, ALU.mult))
            S["gate_dv"] = s_dv.n
            dbg_tap("ygate%d" % c, y_sb, [128, 2, DI])

        def b_yntr(c):
            S = cst[c]
            for b4 in range(4):
                transpose8(
                    [yn_bf[:, i % 2, 128 * (4 * b4 + i // 2):128 * (4 * b4 + i // 2 + 1)]
                     for i in range(8)],
                    y_fm[:, 4 * b4:4 * (b4 + 1), :].rearrange("p a (m t) -> p a m t", m=2),
                    [(s_dv, S["gate_dv"])])
            S["yfm_dv"] = s_ac.n

        def b_outproj(c, ncq):
            S = cst[c]
            cl = c % BLK
            xs = c % 2
            slot, fill = wout_tag[(c, ncq)]
            bi = 2 + ncq % 2
            dep0 = [(s_wo[slot], fill), (s_ac, S["yfm_dv"]), (s_xt, S["x_ld"])] + brel(bi)
            for mt in range(2):
                for kt in range(KI):
                    pe(lambda e, mt=mt, kt=kt, slot=slot, bi=bi: e.matmul(
                        pb[bi][:, 128 * mt:128 * (mt + 1)],
                        y_fm[:, kt, 128 * mt:128 * (mt + 1)],
                        wout_ring[slot][:, kt, :],
                        start=(kt == 0), stop=(kt == KI - 1)),
                       inc=(mt == 1 and kt == KI - 1),
                       deps=(dep0 if (mt == 0 and kt == 0) else []))
            wout_rel[slot] = D(s_pe)
            if ncq + 2 < WOUT_NS:
                load_wout(c, ncq + 2)
            dv(lambda e, ncq=ncq, bi=bi, cl=cl, xs=xs: e.scalar_tensor_tensor(
                x2_blk[:, 2 * cl:2 * cl + 2, 128 * ncq:128 * (ncq + 1)],
                pb[bi][:, 0:256].rearrange("p (m t) -> p m t", m=2), 1.0,
                x_tm[xs][:, :, 128 * ncq:128 * (ncq + 1)], ALU.mult, ALU.add),
               deps=[D(s_pe)])
            bank_rel[bi] = D(s_dv)
            if ncq == WOUT_NS - 1:
                S["r1_dv"] = D(s_dv)
                S["r1n"] = s_dv.n
                dbg_tap("r1_%d" % c, x2_blk, [128, 2 * BLK, DM])

        def b_ln1(c):
            S = cst[c]
            cl = c % BLK
            for mt in range(2):
                tt = 2 * cl + mt
                accum_stat(8, x2_blk[:, tt, :], AF.Square, sq_scr,
                           deps=[(s_dv, S["r1n"])])
                accum_stat(10, x2_blk[:, tt, :], AF.Copy, sq_scr)
                dv(lambda e: e.tensor_scalar_mul(stat[:, 10:11], stat[:, 10:11], 1.0 / DM),
                   deps=[D(s_ac)])
                dv(lambda e: e.tensor_tensor(stat[:, 12:13], stat[:, 10:11],
                                             stat[:, 10:11], ALU.mult), deps=[D(s_dv)])
                dv(lambda e: e.scalar_tensor_tensor(stat[:, 8:9], stat[:, 8:9], 1.0 / DM,
                                                    stat[:, 12:13], ALU.mult, ALU.subtract),
                   deps=[D(s_dv)])
                ac(lambda e: e.activation(stat[:, 8:9], stat[:, 8:9], AF.Ln,
                                          bias=stat[:, 15:16]), deps=[D(s_dv)])
                ac(lambda e: e.activation(stat[:, 8:9], stat[:, 8:9], AF.Exp, scale=-0.5))
                dv(lambda e, tt=tt: e.tensor_scalar(
                    x2_blk[:, tt, :], x2_blk[:, tt, :], stat[:, 10:11],
                    stat[:, 8:9], ALU.subtract, ALU.mult),
                   deps=[D(s_ac)])
            dbg_tap("x2_%d" % c, x2_blk, [128, 2 * BLK, DM])

        def b_x2router(c):
            cl = c % BLK
            dv(lambda e, cl=cl: e.tensor_copy(x2bf, x2_blk[:, 2 * cl:2 * cl + 2, :]),
               deps=[D(s_pe)])
            x2bf_dv = s_dv.n
            for g in range(2):
                transpose8(
                    [x2bf[:, i % 2, 128 * (g * 4 + i // 2):128 * (g * 4 + i // 2 + 1)]
                     for i in range(8)],
                    x2_fm[:, 4 * g:4 * (g + 1), cl * L:(cl + 1) * L]
                        .rearrange("p a (m t) -> p a m t", m=2),
                    [(s_dv, x2bf_dv)])
            x2fm_dv = s_ac.n

            for mt in range(2):
                for kt in range(KD):
                    pe(lambda e, mt=mt, kt=kt, cl=cl: e.matmul(
                        pb[6][:, 16 * mt:16 * mt + NE],
                        x2_fm[:, kt, cl * L + 128 * mt:cl * L + 128 * (mt + 1)],
                        wr_bf[:, kt, :], start=(kt == 0), stop=(kt == KD - 1)),
                       inc=(mt == 1 and kt == KD - 1),
                       deps=(([(s_ac, x2fm_dv), (s_in, init_in)] + brel(6))
                             if (mt == 0 and kt == 0) else []))
            rt_pe = s_pe.n
            for mt in range(2):
                tt = 2 * cl + mt
                dv(lambda e, mt=mt: e.reduce_max(stat[:, 13:14],
                                                 pb[6][:, 16 * mt:16 * mt + NE], axis=AX.X),
                   deps=[(s_pe, rt_pe)])
                dv(lambda e, mt=mt: e.tensor_scalar(probs_p[:, 0:NE],
                                                    pb[6][:, 16 * mt:16 * mt + NE],
                                                    stat[:, 13:14], None, ALU.subtract),
                   deps=[D(s_dv)])
                ac(lambda e: e.activation(probs_p[:, 0:NE], probs_p[:, 0:NE], AF.Exp),
                   deps=[D(s_dv)])
                dv(lambda e: e.max(out=m8_p, in_=probs_p), deps=[D(s_ac)])
                dv(lambda e: e.memset(m8_p[:, 2:8], -2.0), deps=[D(s_dv)])
                dv(lambda e: e.match_replace(out=zap_p, in_to_replace=m8_p,
                                             in_values=probs_p, imm_value=0.0),
                   deps=[D(s_dv)])
                dv(lambda e, tt=tt: e.tensor_tensor(cw_blk[:, tt, :], probs_p, zap_p,
                                                    ALU.subtract), deps=[D(s_dv)])
                dv(lambda e: e.tensor_tensor(stat[:, 13:14], m8_p[:, 0:1], m8_p[:, 1:2],
                                             ALU.add), deps=[D(s_dv)])
                dv(lambda e: e.reciprocal(stat[:, 13:14], stat[:, 13:14]), deps=[D(s_dv)])
                dv(lambda e, tt=tt: e.tensor_scalar(cw_blk[:, tt, :], cw_blk[:, tt, :],
                                                    stat[:, 13:14], None, ALU.mult),
                   deps=[D(s_dv)])
            bank_rel[6] = D(s_dv)
            dbg_tap("cw%d" % c, cw_blk, [128, 2 * BLK, 16])

        wgu_rel = {"g": [None] * 3, "u": [None] * 3}
        wd_rel = [None, None]
        slab_ctr = {"g": 0, "u": 0}
        WG_NS = 11
        gs_rel = [None, None]
        hid_rel = [None]

        def moe_block(blk, phase_deps):
            def load_wgu(which, e_, s, extra=()):
                ring = wg_flat if which == "g" else wu_flat
                sems = s_wg if which == "g" else s_wu
                src = wg_h if which == "g" else wu_h
                slot = slab_ctr[which] % 3
                slab_ctr[which] += 1
                deps = list(extra) + ([wgu_rel[which][slot]] if wgu_rel[which][slot] else [])
                dma("gpsimd", ring[slot],
                    src[e_][:, s * 2048:(s + 1) * 2048], sems[slot], deps=deps)
                return (slot, sems[slot].n)

            wd_tag = {}
            WDP = KF * DM // 8

            def load_wd_piece(e_, k, extra=()):
                slot = e_ % 2
                deps = list(extra) + ([wd_rel[slot]] if wd_rel[slot] else [])
                dma("gpsimd", wd_flat[slot][:, WDP * k:WDP * (k + 1)],
                    wd_h[e_][:, WDP * k:WDP * (k + 1)], s_wd[slot], deps=deps)
                wd_tag[e_] = (slot, s_wd[slot].n)

            pre_g = {}
            pre_u = {}
            for s in range(3):
                pre_g[s] = load_wgu("g", 0, s,
                                    extra=(phase_deps if s == 0 else ()))
                pre_u[s] = load_wgu("u", 0, s)
            for k in range(8):
                load_wd_piece(0, k, extra=(phase_deps if k == 0 else ()))
            for e_ in range(NE):
                wslot, wd_fill = wd_tag[e_]
                for g in range(2):
                    gt = slice(512 * g, 512 * (g + 1))
                    if e_ == 0 and g == 0:
                        tags_g = dict(pre_g)
                        tags_u = dict(pre_u)
                    else:
                        tags_g = {}
                        tags_u = {}
                        for s in range(3):
                            tags_g[s] = load_wgu("g", e_, s)
                            tags_u[s] = load_wgu("u", e_, s)
                    for m in range(KF):
                        s = m // 2
                        mi = m % 2
                        gslot, gfill = tags_g[s]
                        uslot, ufill = tags_u[s]
                        wid0 = 128 * mi
                        gb = m % 2
                        ub = 2 + m % 2
                        gdep = [(s_wg[gslot], gfill)] + brel(gb)
                        if m == 0:
                            gdep += [D(s_dv)]
                        for kt in range(KD):
                            pe(lambda e, kt=kt, gslot=gslot, wid0=wid0, gb=gb, gt=gt: e.matmul(
                                pb[gb], wg_ring[gslot][:, kt, wid0:wid0 + 128],
                                x2_fm[:, kt, gt], start=(kt == 0), stop=(kt == KD - 1)),
                               inc=(kt == KD - 1), deps=(gdep if kt == 0 else []))
                        last_of_slab = (mi == 1)
                        if last_of_slab:
                            wgu_rel["g"][gslot] = D(s_pe)
                            if s + 3 < WG_NS:
                                tags_g[s + 3] = load_wgu("g", e_, s + 3)
                            if g == 0 and e_ + 1 < NE and s < 8:
                                load_wd_piece(e_ + 1, s)
                        acdep = [D(s_pe)] + ([gs_rel[m % 2]] if gs_rel[m % 2] else [])
                        ac(lambda e, m=m, gb=gb: e.activation(gate_s[m % 2], pb[gb], AF.Silu),
                           deps=acdep)
                        bank_rel[gb] = D(s_ac)
                        udep = [(s_wu[uslot], ufill)] + brel(ub)
                        for kt in range(KD):
                            pe(lambda e, kt=kt, uslot=uslot, wid0=wid0, ub=ub, gt=gt: e.matmul(
                                pb[ub], wu_ring[uslot][:, kt, wid0:wid0 + 128],
                                x2_fm[:, kt, gt], start=(kt == 0), stop=(kt == KD - 1)),
                               inc=(kt == KD - 1), deps=(udep if kt == 0 else []))
                        if last_of_slab:
                            wgu_rel["u"][uslot] = D(s_pe)
                            if s + 3 < WG_NS:
                                tags_u[s + 3] = load_wgu("u", e_, s + 3)
                        hdep = [D(s_pe), D(s_ac)] + \
                            ([hid_rel[0]] if (hid_rel[0] and m == 0) else [])
                        dv(lambda e, m=m, ub=ub: e.tensor_tensor(
                            hid_bf[:, m, :], pb[ub], gate_s[m % 2], ALU.mult), deps=hdep)
                        gs_rel[m % 2] = D(s_dv)
                        bank_rel[ub] = D(s_dv)
                    hid_dv = s_dv.n
                    for tq in range(4):
                        tt = 4 * g + tq
                        for nch2 in range(2):
                            bi = 4 + (tq * 2 + nch2) % 3
                            dep0 = [(s_dv, hid_dv), (s_wd[wslot], wd_fill)] + brel(bi)
                            for kt in range(KF):
                                pe(lambda e, tq=tq, nch2=nch2, kt=kt, bi=bi,
                                   wslot=wslot: e.matmul(
                                    pb[bi], hid_bf[:, kt, 128 * tq:128 * (tq + 1)],
                                    wd_buf[wslot][:, kt, 512 * nch2:512 * (nch2 + 1)],
                                    start=(kt == 0), stop=(kt == KF - 1)),
                                   inc=(kt == KF - 1), deps=(dep0 if kt == 0 else []))
                            dv(lambda e, tt=tt, nch2=nch2, bi=bi, e_=e_: e.scalar_tensor_tensor(
                                x2_blk[:, tt, 512 * nch2:512 * (nch2 + 1)], pb[bi],
                                cw_blk[:, tt, e_:e_ + 1],
                                x2_blk[:, tt, 512 * nch2:512 * (nch2 + 1)],
                                ALU.mult, ALU.add),
                               deps=[D(s_pe)])
                            bank_rel[bi] = D(s_dv)
                    hid_rel[0] = D(s_pe)
                wd_rel[wslot] = D(s_pe)

            dbg_tap("moe%d" % blk, x2_blk, [128, 2 * BLK, DM])

            moe_dv = s_dv.n
            for tq in range(2 * BLK):
                accum_stat(8, x2_blk[:, tq, :], AF.Square, dump_m,
                           deps=[(s_dv, moe_dv)])
                accum_stat(10, x2_blk[:, tq, :], AF.Copy, dump_m)
                dv(lambda e: e.tensor_scalar_mul(stat[:, 10:11], stat[:, 10:11], 1.0 / DM),
                   deps=[D(s_ac)])
                dv(lambda e: e.tensor_tensor(stat[:, 12:13], stat[:, 10:11],
                                             stat[:, 10:11], ALU.mult), deps=[D(s_dv)])
                dv(lambda e: e.scalar_tensor_tensor(stat[:, 8:9], stat[:, 8:9], 1.0 / DM,
                                                    stat[:, 12:13], ALU.mult, ALU.subtract),
                   deps=[D(s_dv)])
                ac(lambda e: e.activation(stat[:, 8:9], stat[:, 8:9], AF.Ln,
                                          bias=stat[:, 15:16]), deps=[D(s_dv)])
                ac(lambda e: e.activation(stat[:, 8:9], stat[:, 8:9], AF.Exp, scale=-0.5))
                dv(lambda e, tq=tq: e.tensor_scalar(
                    x2_blk[:, tq, :], x2_blk[:, tq, :], stat[:, 10:11], stat[:, 8:9],
                    ALU.subtract, ALU.mult), deps=[D(s_ac)])
                r0 = blk * BT + tq * 128
                dma("sync", out_h[r0:r0 + 128, :], x2_blk[:, tq, :], s_stt[tq % 2],
                    deps=[D(s_dv)])
                store_tags.append((s_stt[tq % 2], s_stt[tq % 2].n))

        store_tags = []
        pd = fence()
        for blk in range((n_chunks + BLK - 1) // BLK):
            c0 = blk * BLK
            a_dma(c0, pd)
            a_xfm(c0)
            for sl in SLAB_ORDER:
                a_inproj_slab(c0, sl)
            a_dt(c0)
            a_conv(c0)
            for cl in range(BLK):
                c = c0 + cl
                nxt = c + 1 if (cl + 1 < BLK and c + 1 < n_chunks) else None
                if nxt is not None:
                    a_dma(nxt)
                b_head(c)
                b_sloop(c)
                b_gate(c)
                b_yntr(c)
                if nxt is not None:
                    a_xfm(nxt)
                for k in range(10):
                    if k < WOUT_NS:
                        b_outproj(c, k)
                    if nxt is not None:
                        a_inproj_slab(nxt, SLAB_ORDER[k])
                if nxt is not None:
                    a_dt(nxt)
                    a_conv(nxt)
                    for k in range(10, 14):
                        a_inproj_slab(nxt, SLAB_ORDER[k])
                b_ln1(c)
                if nxt is not None:
                    for k in range(14, WIN_NS):
                        a_inproj_slab(nxt, SLAB_ORDER[k])
                b_x2router(c)
                if c >= n_chunks - 1:
                    break
            pd = fence()
            if do_moe:
                moe_block(blk, pd)
                pd = fence() + [D(s_stt[0]), D(s_stt[1])]

        P.wait("sync", s_sp, s_sp.n)
        P.wait("sync", s_stt[0], s_stt[0].n)
        P.wait("sync", s_stt[1], s_stt[1].n)
        P.wait("sync", s_xt, s_xt.n)
        P.wait("vector", s_stt[0], s_stt[0].n)
        P.wait("vector", s_stt[1], s_stt[1].n)

        blkctx = st.enter_context(nc.Block())
        P.emit(blkctx)

    return nc, dbg_h


def _consts():
    idt_bf = np.eye(128, dtype=ml_dtypes.bfloat16)
    idt_f = np.eye(128, dtype=np.float32)
    ones_r = np.ones((1, 128), dtype=np.float32)
    mask = np.zeros((2, 128, 512), dtype=np.float32)
    for jt in range(2):
        for j in range(128):
            jg = jt * 128 + j
            m = np.where(np.arange(L) >= jg, 0.0, -MASKVAL).astype(np.float32)
            mask[jt, j, 0:L] = m
            mask[jt, j, L:2 * L] = m
    return idt_bf, idt_f, ones_r, mask


def _slabify(W, widths):
    K, N = W.shape
    kd = K // 128
    Wr = np.asarray(W, np.float32).reshape(kd, 128, N).transpose(1, 0, 2)
    parts, col = [], 0
    for w in widths:
        parts.append(Wr[:, :, col:col + w].reshape(128, kd * w))
        col += w
    return np.ascontiguousarray(np.concatenate(parts, axis=1))


def make_in_maps(inputs, ncores=8):
    x = np.ascontiguousarray(inputs["L0_attn_hidden_states"], dtype=np.float32)
    per = x.shape[0] // ncores
    idt_bf, idt_f, ones_r, mask = _consts()
    wg_f = np.asarray(inputs["L0_ffn_Wg"], np.float32)
    wu_f = np.asarray(inputs["L0_ffn_Wu"], np.float32)
    wd_f = np.asarray(inputs["L0_ffn_Wd"], np.float32)
    wgu_widths = [256] * 11
    common = dict(
        w_in=_slabify(np.asarray(inputs["L0_attn_W_in"], np.float32),
                      [256] * 17 + [32]),
        conv_w=np.ascontiguousarray(inputs["L0_attn_conv_w"], np.float32),
        conv_b=np.ascontiguousarray(inputs["L0_attn_conv_b"], np.float32),
        dt_bias=np.ascontiguousarray(inputs["L0_attn_dt_bias"], np.float32),
        a_log=np.ascontiguousarray(inputs["L0_attn_A_log"], np.float32),
        dd=np.ascontiguousarray(inputs["L0_attn_D"], np.float32),
        norm_w=np.ascontiguousarray(inputs["L0_attn_norm_w"], np.float32),
        w_out=_slabify(np.asarray(inputs["L0_attn_W_out"], np.float32),
                       [128] * 8),
        wr=np.ascontiguousarray(inputs["L0_ffn_Wr"], np.float32),
        wg=np.stack([_slabify(wg_f[e], wgu_widths) for e in range(NE)]),
        wu=np.stack([_slabify(wu_f[e], wgu_widths) for e in range(NE)]),
        wd=np.stack([_slabify(wd_f[e], [DM]) for e in range(NE)]),
        idt_bf=idt_bf, idt_f=idt_f, ones_r=ones_r, maskbig=mask,
    )
    in_maps = []
    for ci in range(ncores):
        m = dict(common)
        m["x"] = np.ascontiguousarray(x[ci * per:(ci + 1) * per].reshape(T, DM))
        in_maps.append(m)
    return in_maps, per


def kernel(L0_attn_hidden_states, L0_attn_W_in, L0_attn_conv_w, L0_attn_conv_b,
           L0_attn_dt_bias, L0_attn_A_log, L0_attn_D, L0_attn_norm_w, L0_attn_W_out,
           L0_ffn_Wr, L0_ffn_Wg, L0_ffn_Wu, L0_ffn_Wd):
    inputs = dict(
        L0_attn_hidden_states=L0_attn_hidden_states, L0_attn_W_in=L0_attn_W_in,
        L0_attn_conv_w=L0_attn_conv_w, L0_attn_conv_b=L0_attn_conv_b,
        L0_attn_dt_bias=L0_attn_dt_bias, L0_attn_A_log=L0_attn_A_log,
        L0_attn_D=L0_attn_D, L0_attn_norm_w=L0_attn_norm_w,
        L0_attn_W_out=L0_attn_W_out, L0_ffn_Wr=L0_ffn_Wr, L0_ffn_Wg=L0_ffn_Wg,
        L0_ffn_Wu=L0_ffn_Wu, L0_ffn_Wd=L0_ffn_Wd)
    in_maps, per = make_in_maps(inputs)
    nc, _ = build()
    res = run_bass_kernel_spmd(nc, in_maps, core_ids=list(range(8)))
    outs = [r["out"].reshape(per, T // per, DM) for r in res.results]
    return np.concatenate(outs, axis=0)


# version 15
# speedup vs baseline: 1.0267x; 1.0267x over previous
import numpy as np
import ml_dtypes
from contextlib import ExitStack

import concourse.bass as bass
import concourse.mybir as mybir
from concourse.bass_utils import run_bass_kernel_spmd

F32 = mybir.dt.float32
BF16 = mybir.dt.bfloat16
AF = mybir.ActivationFunctionType
ALU = mybir.AluOpType
AX = mybir.AxisListType

DM, DI, NH, HD, DS = 1024, 2048, 32, 64, 128
CONVD = DI + 2 * DS
DINPROJ = 2 * DI + 2 * DS + NH
NE, DF = 9, 2816
EPS = 1e-5

L, NCH = 256, 16
T = L * NCH
BLK = 4
NBLK = NCH // BLK
BT = L * BLK
SEQ_CHUNKS = 8
KD, KI, KF, CM = DM // 128, DI // 128, DF // 128, CONVD // 128
MASKVAL = 1e9
ACCUM_RESETS = True


class Sem:
    def __init__(self, st, nc, name):
        self.h = st.enter_context(nc.semaphore(name))
        self.n = 0
        self.name = name


class Prog:
    def __init__(self):
        self.ops = {k: [] for k in ("sync", "tensor", "vector", "scalar", "gpsimd")}

    def wait(self, eng, sem, val=None):
        v = sem.n if val is None else val
        if v > 0:
            self.ops[eng].append(("w", sem, v))

    def op(self, eng, fn, incs=()):
        for s, k in incs:
            s.n += k
        self.ops[eng].append(("o", fn, incs))

    def emit(self, block):
        def make(name):
            def body(eng):
                hi = {}
                for item in self.ops[name]:
                    if item[0] == "w":
                        _, sem, v = item
                        if hi.get(sem.name, -1) >= v:
                            continue
                        hi[sem.name] = v
                        eng.wait_ge(sem.h, v)
                    else:
                        _, fn, incs = item
                        inst = fn(eng)
                        for s, k in incs:
                            inst.then_inc(s.h, k)
            return body
        block.sync(make("sync"))
        block.tensor(make("tensor"))
        block.vector(make("vector"))
        block.scalar(make("scalar"))
        block.gpsimd(make("gpsimd"))


def build(n_chunks=NCH, dbg=(), do_moe=True):
    nc = bass.Bass("TRN2", target_bir_lowering=False, debug=False)
    dbg = set(dbg)

    x_h = nc.dram_tensor("x", [T, DM], F32, kind="ExternalInput").ap()
    w_in_h = nc.dram_tensor("w_in", [128, 17 * 2048 + 256], F32, kind="ExternalInput").ap()
    conv_w_h = nc.dram_tensor("conv_w", [CONVD, 4], F32, kind="ExternalInput").ap()
    conv_b_h = nc.dram_tensor("conv_b", [CONVD], F32, kind="ExternalInput").ap()
    dt_bias_h = nc.dram_tensor("dt_bias", [NH], F32, kind="ExternalInput").ap()
    a_log_h = nc.dram_tensor("a_log", [NH], F32, kind="ExternalInput").ap()
    d_h = nc.dram_tensor("dd", [NH], F32, kind="ExternalInput").ap()
    normw_h = nc.dram_tensor("norm_w", [DI], F32, kind="ExternalInput").ap()
    w_out_h = nc.dram_tensor("w_out", [128, 8 * 2048], F32, kind="ExternalInput").ap()
    wr_h = nc.dram_tensor("wr", [DM, NE], F32, kind="ExternalInput").ap()
    wg_h = nc.dram_tensor("wg", [NE, 128, KD * DF], F32, kind="ExternalInput").ap()
    wu_h = nc.dram_tensor("wu", [NE, 128, KD * DF], F32, kind="ExternalInput").ap()
    wd_h = nc.dram_tensor("wd", [NE, 128, KF * DM], F32, kind="ExternalInput").ap()
    idtb_h = nc.dram_tensor("idt_bf", [128, 128], BF16, kind="ExternalInput").ap()
    idtf_h = nc.dram_tensor("idt_f", [128, 128], F32, kind="ExternalInput").ap()
    ones_h = nc.dram_tensor("ones_r", [1, 128], F32, kind="ExternalInput").ap()
    maskbig_h = nc.dram_tensor("maskbig", [2, 128, 512], F32, kind="ExternalInput").ap()

    out_h = nc.dram_tensor("out", [T, DM], F32, kind="ExternalOutput").ap()
    dbg_h = {}
    a_scr_h = nc.dram_tensor("a_scr", [2, 2, NH * L], BF16, kind="Internal").ap()

    st = ExitStack()
    with st:
        SBt = lambda name, shape, dt=F32: st.enter_context(nc.sbuf_tensor(name, shape, dt)).ap()
        PSt = lambda name, shape, dt=F32: st.enter_context(nc.psum_tensor(name, shape, dt)).ap()

        idt_bf = SBt("idt_bf_s", [128, 128], BF16)
        idt_f = SBt("idt_f_s", [128, 128], F32)
        ones_r = SBt("ones_r_s", [1, 128], F32)
        ones_bf = SBt("ones_bf_s", [1, 128], BF16)
        maskdup = SBt("maskdup_s", [128, 2, 512], BF16)
        conv_w = SBt("conv_wc", [128, CM, 4], BF16)
        conv_b = SBt("conv_bc", [128, CM], BF16)
        dt_bias = SBt("dt_biasc", [NH, 1], F32)
        a_neg = SBt("a_neg", [NH, 1], F32)
        d_bc = SBt("d_bc", [128, NH], F32)
        normw_bc = SBt("normw_bc", [128, DI], BF16)
        wr_bf = SBt("wr_bf", [128, KD, NE], BF16)
        stat = SBt("stat_c", [128, 16], F32)
        s_f32 = SBt("s_f32", [128, NH * HD], F32)
        s_bfb = SBt("s_bfb", [128, NH * HD], BF16)
        halo = SBt("halo", [128, CM, 3], BF16)
        x2_blk = SBt("x2_blk", [128, 2 * BLK, DM], F32)
        x2_fm = SBt("x2_fm", [128, KD, BT], BF16)
        cw_blk = SBt("cw_blk", [128, 2 * BLK, 16], F32)
        probs_p = SBt("probs_p", [128, 16], F32)
        m8_p = SBt("m8_p", [128, 8], F32)
        zap_p = SBt("zap_p", [128, 16], F32)
        al_row = SBt("al_row", [1, NH], F32)
        b_tm = SBt("b_tm", [128, 2, 128], BF16)

        AR = 141632
        arena = st.enter_context(nc.sbuf_tensor("arena", [128, AR], mybir.dt.uint8)).ap()

        class Alloc:
            def __init__(self):
                self.off = 0

            def take(self, nbytes, align=64):
                o = (self.off + align - 1) // align * align
                self.off = o + nbytes
                assert self.off <= AR, f"arena overflow {self.off}"
                return o

        def view(off, shape, dt, pdim=128):
            isz = mybir.dt.size(dt)
            n = 1
            for s in shape[1:]:
                n *= s
            ap = arena[0:pdim, off:off + n * isz].bitcast(dt)
            if len(shape) == 2:
                return ap
            if len(shape) == 3:
                return ap.rearrange("p (a b) -> p a b", a=shape[1])
            raise ValueError(shape)

        A = Alloc()
        x_tm = [view(A.take(2 * DM * 4), [128, 2, DM], F32) for _ in range(2)]
        xbf = view(A.take(2 * DM * 2), [128, 2, DM], BF16)
        x_fm = view(A.take(KD * L * 2), [128, KD, L], BF16)
        z_silu = view(A.take(2 * DI * 2), [128, 2, DI], BF16)
        conv_in = view(A.take(CM * (3 + L) * 2), [128, CM, 3 + L], BF16)
        o_ct = A.take(max(2 * 9 * L * 2, DI * 4))
        conv_t1 = view(o_ct, [128, 9, L], BF16)
        conv_t2 = view(o_ct + 9 * L * 2, [128, 9, L], BF16)
        tmp2k = view(o_ct, [128, DI], F32)
        sq_scr = view(o_ct, [128, DM], F32)
        conv_s = view(A.take(CM * L * 2), [128, CM, L], BF16)
        dt_raw = view(A.take(L * 4), [NH, L], F32, pdim=NH)
        af_bf = [view(A.take(1024 * 2), [1, 1024], BF16, pdim=1) for _ in range(3)]
        a_hi_bf = view(A.take(L * 2), [NH, L], BF16, pdim=NH)
        a_lo_bf = view(A.take(L * 2), [NH, L], BF16, pdim=NH)
        dt_w0 = view(A.take(L * 4), [NH, L], F32, pdim=NH)
        dt_w1 = view(A.take(L * 4), [NH, L], F32, pdim=NH)
        a_fm = view(A.take(L * 4), [NH, L], F32, pdim=NH)
        ds_fm = view(A.take(L * 4), [NH, L], F32, pdim=NH)
        a_tm = view(A.take(2 * NH * 4), [128, 2, NH], F32)
        a_tm_neg = view(A.take(2 * NH * 4), [128, 2, NH], F32)
        ea_tm = view(A.take(2 * NH * 4), [128, 2, NH], F32)
        ds_tm = view(A.take(2 * NH * 4), [128, 2, NH], F32)
        dt_tm = view(A.take(2 * NH * 4), [128, 2, NH], F32)
        cd_bc = view(A.take(NH * 4), [128, NH], F32)
        o_xs = A.take(2 * DI * 2)
        xssm_tm = view(o_xs, [128, 2, DI], BF16)
        y_fm = view(o_xs, [128, KI, L], BF16)
        o_xt = A.take(2 * DI * 2)
        xt_tm = view(o_xt, [128, 2, DI], BF16)
        yn_bf = view(o_xt, [128, 2, DI], BF16)
        o_xd = A.take(2 * DI * 2)
        xtds_tm = view(o_xd, [128, 2, DI], BF16)
        x2bf = view(o_xd, [128, 2, DM], BF16)
        cbt_bf = view(A.take(2 * L * 2), [128, 2, L], BF16)
        s_t = [view(A.take(2 * L * 2), [128, 2, L], BF16) for _ in range(2)]
        y_sb = view(A.take(2 * DI * 4), [128, 2, DI], F32)
        o_wi = [A.take(KD * 256 * 2) for _ in range(3)]
        win_ring = [view(o, [128, KD, 256], BF16) for o in o_wi]
        win_ring32 = [view(o, [128, KD, 32], BF16) for o in o_wi]
        win_flat = [view(o, [128, KD * 256], BF16) for o in o_wi]
        o_wo = [A.take(KI * 128 * 2) for _ in range(2)]
        wout_ring = [view(o, [128, KI, 128], BF16) for o in o_wo]
        wout_flat = [view(o, [128, KI * 128], BF16) for o in o_wo]
        stag = view(0, [1, DI], F32, pdim=1)
        mamba_bytes = A.off

        A = Alloc()
        o_wg = [A.take(KD * 256 * 2) for _ in range(3)]
        wg_ring = [view(o, [128, KD, 256], BF16) for o in o_wg]
        wg_flat = [view(o, [128, KD * 256], BF16) for o in o_wg]
        o_wu = [A.take(KD * 256 * 2) for _ in range(3)]
        wu_ring = [view(o, [128, KD, 256], BF16) for o in o_wu]
        wu_flat = [view(o, [128, KD * 256], BF16) for o in o_wu]
        o_wd = [A.take(KF * DM * 2) for _ in range(2)]
        wd_buf = [view(o, [128, KF, DM], BF16) for o in o_wd]
        wd_flat = [view(o, [128, KF * DM], BF16) for o in o_wd]
        o_hid = A.take(KF * 512 * 2)
        hid_bf = view(o_hid, [128, KF, 512], BF16)
        dump_m = view(o_hid, [128, DM], F32)
        gate_s = [view(A.take(512 * 2), [128, 512], BF16) for _ in range(2)]
        moe_bytes = A.off
        assert max(mamba_bytes, moe_bytes) <= AR, (mamba_bytes, moe_bytes)

        pb = [PSt(f"pb{i}", [128, 512], F32) for i in range(7)]
        ptr = PSt("ptr", [128, 1024], BF16)

        s_sp = Sem(st, nc, "s_sp")
        s_pe = Sem(st, nc, "s_pe")
        s_dv = Sem(st, nc, "s_dv")
        s_ac = Sem(st, nc, "s_ac")
        s_xb = Sem(st, nc, "s_xb")
        s_xt = Sem(st, nc, "s_xt")
        s_as = Sem(st, nc, "s_as")
        s_af = [Sem(st, nc, f"s_af{i}") for i in range(3)]
        s_stt = [Sem(st, nc, "s_st0"), Sem(st, nc, "s_st1")]
        s_wi = [Sem(st, nc, f"s_wi{i}") for i in range(3)]
        s_wo = [Sem(st, nc, f"s_wo{i}") for i in range(2)]
        s_wg = [Sem(st, nc, f"s_wg{i}") for i in range(3)]
        s_wu = [Sem(st, nc, f"s_wu{i}") for i in range(3)]
        s_wd = [Sem(st, nc, f"s_wd{i}") for i in range(2)]
        s_in = Sem(st, nc, "s_in")
        s_tap = Sem(st, nc, "s_tap")

        P = Prog()
        bank_rel = [None] * 8

        def dma(q, out, in_, sem, deps=(), slow=False):
            for dsem, dval in deps:
                P.wait(q, dsem, dval)
            P.op(q, lambda e, o=out, i=in_, s=slow: e.dma_start(
                out=o, in_=i, allow_slow_non_contiguous=s), [(sem, 16)])

        def pe(fn, inc=False, deps=()):
            for dsem, dval in deps:
                P.wait("tensor", dsem, dval)
            P.op("tensor", fn, [(s_pe, 1)] if inc else ())

        def dv(fn, inc=True, deps=()):
            P.wait("vector", s_dv)
            for dsem, dval in deps:
                P.wait("vector", dsem, dval)
            P.op("vector", fn, [(s_dv, 1)] if inc else ())

        def ac(fn, inc=True, deps=()):
            P.wait("scalar", s_ac)
            for dsem, dval in deps:
                P.wait("scalar", dsem, dval)
            P.op("scalar", fn, [(s_ac, 1)] if inc else ())

        def D(sem):
            return (sem, sem.n)

        def fence():
            return [D(s_pe), D(s_dv), D(s_ac)]

        def brel(i):
            return [bank_rel[i]] if bank_rel[i] else []

        def dbg_tap(name, ap, shape, dt=F32):
            if name not in dbg:
                return
            h = nc.dram_tensor("dbg_" + name, list(shape), dt, kind="ExternalOutput").ap()
            dbg_h[name] = h
            prev = s_tap.n
            dma("sync", h, ap, s_tap, deps=fence() + [(s_tap, prev)])
            for eng in ("vector", "scalar", "tensor", "gpsimd"):
                P.wait(eng, s_tap, s_tap.n)

        def accum_stat(col, in_ap, func, dump, deps=()):
            if ACCUM_RESETS:
                dv(lambda e, col=col: e.memset(stat[:, col:col + 1], 0.0),
                   deps=list(deps))
                deps = [D(s_dv)]
            ac(lambda e, col=col, in_ap=in_ap, func=func, dump=dump: e.activation(
                dump, in_ap, func, accum_out=stat[:, col:col + 1]), deps=list(deps))

        dma("sync", idt_bf, idtb_h, s_sp)
        dma("sync", idt_f, idtf_h, s_sp)
        dma("sync", ones_r, ones_h, s_sp)
        dma("gpsimd", maskdup, maskbig_h.rearrange("j p n -> p j n"), s_in)
        dma("sync", dt_bias, dt_bias_h.unsqueeze(1), s_sp)
        dma("sync", a_neg, a_log_h.unsqueeze(1), s_sp)
        dma("sync", stag, normw_h.unsqueeze(0), s_sp)
        ld1 = s_sp.n
        dma("gpsimd", conv_w, conv_w_h.rearrange("(m p) k -> p m k", p=128), s_in)
        dma("gpsimd", conv_b, conv_b_h.rearrange("(m p) -> p m", p=128), s_in, slow=True)
        dma("gpsimd", wr_bf, wr_h.rearrange("(k p) e -> p k e", p=128), s_in)
        init_in = s_in.n

        ac(lambda e: e.activation(a_neg, a_neg, AF.Exp), deps=[(s_sp, ld1)])
        dv(lambda e: e.tensor_scalar_mul(a_neg, a_neg, -1.0), deps=[D(s_ac)])
        dv(lambda e: e.tensor_copy(ones_bf, ones_r), deps=[(s_sp, ld1)])
        for i in range(4):
            pe(lambda e, i=i: e.matmul(pb[0][:, 0:512], ones_r,
                                       stag[0:1, 512 * i:512 * (i + 1)], start=True, stop=True),
               inc=True, deps=[(s_sp, ld1)] + ([D(s_dv)] if i else []))
            dv(lambda e, i=i: e.tensor_copy(normw_bc[:, 512 * i:512 * (i + 1)], pb[0][:, 0:512]),
               deps=[D(s_pe)])
        dma("sync", stag[0:1, 0:NH], d_h.unsqueeze(0), s_sp, deps=[D(s_pe)])
        pe(lambda e: e.matmul(pb[0][:, 0:NH], ones_r, stag[0:1, 0:NH], start=True, stop=True),
           inc=True, deps=[D(s_sp), D(s_dv)])
        dv(lambda e: e.tensor_copy(d_bc, pb[0][:, 0:NH]), deps=[D(s_pe)])
        dv(lambda e: e.memset(s_f32, 0.0))
        dv(lambda e: e.memset(x2_blk, 0.0))
        dv(lambda e: e.memset(cw_blk, 0.0))
        dv(lambda e: e.memset(s_bfb, 0.0))
        dv(lambda e: e.memset(halo, 0.0))
        dv(lambda e: e.memset(probs_p, -1.0))
        dv(lambda e: e.memset(stat[:, 15:16], EPS))
        dv(lambda e: e.memset(stat[:, 14:15], 1.0))
        bank_rel[0] = D(s_dv)

        WIN_NS, WOUT_NS = 18, 8
        win_rel = [None, None, None]
        wout_rel = [None, None]
        win_tag = {}
        wout_tag = {}

        def load_win(c, sl, extra=()):
            slot = (c * WIN_NS + sl) % 3
            wid = 32 if sl == 17 else 256
            deps = list(extra) + ([win_rel[slot]] if win_rel[slot] else [])
            dma("gpsimd", win_flat[slot][:, 0:KD * wid],
                w_in_h[:, sl * 2048:sl * 2048 + KD * wid], s_wi[slot], deps=deps)
            win_tag[(c, sl)] = (slot, s_wi[slot].n)

        def load_wout(c, sl, extra=()):
            slot = (c * WOUT_NS + sl) % 2
            deps = list(extra) + ([wout_rel[slot]] if wout_rel[slot] else [])
            dma("gpsimd", wout_flat[slot], w_out_h[:, sl * 2048:(sl + 1) * 2048],
                s_wo[slot], deps=deps)
            wout_tag[(c, sl)] = (slot, s_wo[slot].n)

        def transpose8(srcs, dst_ap, first_deps):
            deps = list(first_deps) + brel(7)
            for i, s in enumerate(srcs):
                pe(lambda e, i=i, s=s: e.transpose(ptr[:, 128 * i:128 * (i + 1)], s, idt_bf),
                   inc=(i == 7), deps=(deps if i == 0 else []))
            ac(lambda e, d=dst_ap: e.activation(
                d, ptr.rearrange("p (a m t) -> p a m t", a=4, m=2), AF.Copy),
               deps=[D(s_pe)])
            bank_rel[7] = D(s_ac)

        prev_scan = {"pe": 0, "dv": 0}
        SLAB_ORDER = list(range(8, 17)) + [17] + list(range(8))

        cst = {}
        st_rel = [None, None]

        def a_dma(c, phase_deps=()):
            xs = c % 2
            xr = x_h[c * L:(c + 1) * L, :].rearrange("(m p) n -> p m n", p=128)
            xdeps = list(phase_deps)
            if c - 2 in cst:
                xdeps.append(cst[c - 2]["r1_dv"])
            dma("sync", x_tm[xs], xr, s_xt, deps=xdeps)
            bdeps = list(phase_deps)
            if c - 1 in cst and "xfm_pe" in cst[c - 1]:
                bdeps.append(cst[c - 1]["xfm_pe"])
            dma("gpsimd", xbf, xr, s_xb, deps=bdeps)
            cst[c] = {"x_ld": s_xt.n, "xb_ld": s_xb.n}
            for j in range(3):
                load_win(c, SLAB_ORDER[j])

        def a_xfm(c):
            S = cst[c]
            seq_start = (c % SEQ_CHUNKS) == 0
            tdeps = [(s_xb, S["xb_ld"])]
            for g in range(2):
                transpose8(
                    [xbf[:, i % 2, 128 * (g * 4 + i // 2):128 * (g * 4 + i // 2 + 1)]
                     for i in range(8)],
                    x_fm[:, 4 * g:4 * (g + 1), :].rearrange("p a (m t) -> p a m t", m=2),
                    (tdeps if g == 0 else []))
            S["xfm_dv"] = s_ac.n
            S["xfm_pe"] = D(s_pe)
            if seq_start:
                dv(lambda e: e.memset(conv_in[:, :, 0:3], 0.0))
            else:
                dv(lambda e: e.tensor_copy(conv_in[:, :, 0:3], halo))

        def a_inproj_slab(c, sl):
            S = cst[c]
            slot, fill = win_tag[(c, sl)]
            bi = sl % 2
            bank = pb[bi]
            dep0 = [(s_wi[slot], fill), (s_ac, S["xfm_dv"])] + brel(bi)
            if sl < 8:
                for mt in range(2):
                    for kt in range(KD):
                        pe(lambda e, mt=mt, kt=kt, slot=slot, bank=bank: e.matmul(
                            bank[:, 256 * mt:256 * (mt + 1)],
                            x_fm[:, kt, 128 * mt:128 * (mt + 1)],
                            win_ring[slot][:, kt, 0:256],
                            start=(kt == 0), stop=(kt == KD - 1)),
                           inc=(mt == 1 and kt == KD - 1),
                           deps=(dep0 if (mt == 0 and kt == 0) else []))
                win_rel[slot] = D(s_pe)
                j = SLAB_ORDER.index(sl)
                if j + 3 < WIN_NS:
                    load_win(c, SLAB_ORDER[j + 3])
                if j == WIN_NS - 1:
                    S["ip_pe"] = D(s_pe)
                acdeps = [D(s_pe)]
                if c - 1 in cst and "gate_dv" in cst[c - 1]:
                    acdeps.append((s_dv, cst[c - 1]["gate_dv"]))
                ac(lambda e, sl=sl, bank=bank: e.activation(
                    z_silu[:, :, 256 * sl:256 * (sl + 1)],
                    bank.rearrange("p (m t) -> p m t", m=2), AF.Silu),
                   deps=acdeps)
                bank_rel[bi] = D(s_ac)
            elif sl < 17:
                m0 = 2 * (sl - 8)
                for mi in range(2):
                    for kt in range(KD):
                        pe(lambda e, mi=mi, kt=kt, slot=slot, bank=bank: e.matmul(
                            bank[:, 256 * mi:256 * (mi + 1)],
                            win_ring[slot][:, kt, 128 * mi:128 * (mi + 1)],
                            x_fm[:, kt, :],
                            start=(kt == 0), stop=(kt == KD - 1)),
                           inc=(mi == 1 and kt == KD - 1),
                           deps=(dep0 if (mi == 0 and kt == 0) else []))
                win_rel[slot] = D(s_pe)
                j = SLAB_ORDER.index(sl)
                if j + 3 < WIN_NS:
                    load_win(c, SLAB_ORDER[j + 3])
                dv(lambda e, m0=m0, bank=bank: e.tensor_copy(
                    conv_in[:, m0:m0 + 2, 3:3 + L],
                    bank.rearrange("p (m t) -> p m t", m=2)),
                   deps=[D(s_pe)])
                bank_rel[bi] = D(s_dv)
                if sl == 16:
                    S["convin_dv"] = s_dv.n
            else:
                for kt in range(KD):
                    pe(lambda e, kt=kt, slot=slot, bank=bank: e.matmul(
                        bank[0:NH, 0:L], win_ring32[slot][:, kt, 0:NH], x_fm[:, kt, :],
                        start=(kt == 0), stop=(kt == KD - 1)),
                       inc=(kt == KD - 1), deps=(dep0 if kt == 0 else []))
                win_rel[slot] = D(s_pe)
                j = SLAB_ORDER.index(sl)
                if j + 3 < WIN_NS:
                    load_win(c, SLAB_ORDER[j + 3])
                dv(lambda e, bank=bank: e.tensor_copy(dt_raw, bank[0:NH, 0:L]),
                   deps=[D(s_pe)])
                bank_rel[bi] = D(s_dv)
                S["dtraw_dv"] = s_dv.n

        def a_conv(c):
            S = cst[c]
            for h2 in range(2):
                cs = slice(9 * h2, 9 * h2 + 9)
                dv(lambda e, cs=cs: e.tensor_tensor(
                    conv_t1, conv_in[:, cs, 0:L],
                    conv_w[:, cs, 0:1].broadcast_to([128, 9, L]), ALU.mult),
                   deps=([(s_dv, S["convin_dv"]), (s_in, init_in)] if h2 == 0
                         else [D(s_ac)]))
                for k in range(1, 4):
                    dv(lambda e, cs=cs, k=k: e.tensor_tensor(
                        conv_t2, conv_in[:, cs, k:k + L],
                        conv_w[:, cs, k:k + 1].broadcast_to([128, 9, L]), ALU.mult))
                    dv(lambda e: e.tensor_tensor(conv_t1, conv_t1, conv_t2, ALU.add))
                dv(lambda e, cs=cs: e.tensor_tensor(
                    conv_t1, conv_t1,
                    conv_b[:, cs].unsqueeze(2).broadcast_to([128, 9, L]), ALU.add))
                acdeps = [D(s_dv)]
                if h2 == 0 and c - 1 in cst and "cs_free" in cst[c - 1]:
                    acdeps.append(cst[c - 1]["cs_free"])
                ac(lambda e, cs=cs: e.activation(conv_s[:, cs, :], conv_t1, AF.Silu),
                   deps=acdeps)
            S["conv_ac"] = s_ac.n
            dbg_tap("conv%d" % c, conv_s, [128, CM, L], BF16)
            dbg_tap("zs%d" % c, z_silu, [128, 2, DI], BF16)
            dv(lambda e: e.tensor_copy(halo, conv_in[:, :, L:L + 3]))

        def a_dt(c):
            S = cst[c]
            dv(lambda e: e.tensor_scalar(dt_w0, dt_raw, dt_bias, None, ALU.add),
               deps=[(s_dv, S["dtraw_dv"])])
            dv(lambda e: e.scalar_tensor_tensor(dt_w1, dt_w0, -1.0, dt_w0, ALU.mult, ALU.max),
               deps=[D(s_dv)])
            ac(lambda e: e.activation(dt_w1, dt_w1, AF.Exp, scale=-1.0), deps=[D(s_dv)])
            ac(lambda e: e.activation(dt_w1, dt_w1, AF.Ln, bias=stat[0:NH, 14:15]))
            dv(lambda e: e.tensor_scalar(dt_w0, dt_w0, 0.0, None, ALU.max), deps=[D(s_dv)])
            dv(lambda e: e.tensor_tensor(dt_w0, dt_w0, dt_w1, ALU.add),
               deps=[D(s_ac), D(s_dv)])
            dv(lambda e: e.tensor_scalar(dt_w1, dt_w0, a_neg, None, ALU.mult),
               deps=[D(s_dv)])
            dv(lambda e: e.tensor_tensor_scan(a_fm, dt_w1, dt_w1, 0.0, ALU.add, ALU.bypass),
               deps=[D(s_dv)])
            dv(lambda e: e.tensor_scalar(ds_fm, a_fm, a_fm[:, L - 1:L], None, ALU.subtract),
               deps=[D(s_dv)])
            ac(lambda e: e.activation(ds_fm, ds_fm, AF.Exp, scale=-1.0), deps=[D(s_dv)])

            dv(lambda e: e.tensor_copy(a_hi_bf, a_fm), deps=[D(s_ac)])
            dv(lambda e: e.tensor_copy(dt_w1, a_hi_bf))
            dv(lambda e: e.tensor_tensor(dt_w1, a_fm, dt_w1, ALU.subtract))
            dv(lambda e: e.tensor_copy(a_lo_bf, dt_w1))
            asl = a_scr_h[c % 2]
            dma("sync", asl[0].rearrange("(h l) -> h l", h=NH), a_hi_bf, s_as,
                deps=[D(s_dv), D(s_pe)])
            dma("sync", asl[1].rearrange("(h l) -> h l", h=NH), a_lo_bf, s_as)
            S["ascr_ld"] = s_as.n
            S["dtpipe_dv"] = s_dv.n
            S["dtpipe_ac"] = s_ac.n

        def a_dt_tr(c):
            S = cst[c]
            tr_deps = [(s_ac, S["dtpipe_ac"]), (s_dv, S["dtpipe_dv"])] + brel(6)
            for idx, src in enumerate((dt_w0, a_fm, ds_fm)):
                for jt in range(2):
                    pe(lambda e, src=src, jt=jt, idx=idx: e.transpose(
                        pb[6][:, 64 * idx + 32 * jt:64 * idx + 32 * (jt + 1)],
                        src[:, 128 * jt:128 * (jt + 1)], idt_f[0:NH, 0:NH]),
                       inc=(idx == 2 and jt == 1),
                       deps=(tr_deps if (idx == 0 and jt == 0) else []))
            for idx, dst in enumerate((dt_tm, a_tm, ds_tm)):
                dv(lambda e, idx=idx, dst=dst: e.tensor_copy(
                    dst, pb[6][:, 64 * idx:64 * (idx + 1)].rearrange("p (j h) -> p j h", j=2)),
                   deps=[D(s_pe)])
            bank_rel[6] = D(s_dv)
            dv(lambda e: e.tensor_scalar_mul(a_tm_neg, a_tm, -1.0), deps=[D(s_ac)])
            ac(lambda e: e.activation(ea_tm, a_tm, AF.Exp), deps=[D(s_dv)])
            pe(lambda e: e.transpose(pb[6][0:1, 0:NH], a_fm[:, L - 1:L],
                                     idt_f[0:NH, 0:NH]),
               inc=True, deps=brel(6))
            dv(lambda e: e.tensor_copy(al_row[0:1, 0:NH], pb[6][0:1, 0:NH]),
               deps=[D(s_pe)])
            pe(lambda e: e.matmul(pb[6][:, 0:NH], ones_r, al_row, start=True, stop=True),
               inc=True, deps=[D(s_dv)])
            ac(lambda e: e.activation(cd_bc, pb[6][:, 0:NH], AF.Exp), deps=[D(s_pe), D(s_dv)])
            bank_rel[6] = D(s_ac)

        def b_head(c):
            S = cst[c]
            seq_start = (c % SEQ_CHUNKS) == 0
            conv_ac = S["conv_ac"]
            for sl in range(2):
                load_wout(c, sl)
            for b4 in range(4):
                transpose8(
                    [conv_s[:, 4 * b4 + i // 2, 128 * (i % 2):128 * (i % 2 + 1)]
                     for i in range(8)],
                    xssm_tm[:, :, 512 * b4:512 * (b4 + 1)]
                        .rearrange("p m (a t) -> p a m t", a=4),
                    [(s_ac, conv_ac)])
            a_dt_tr(c)
            bdeps = [(s_ac, conv_ac)] + brel(7)
            for jt in range(2):
                pe(lambda e, jt=jt: e.transpose(ptr[:, 128 * jt:128 * (jt + 1)],
                                                conv_s[:, 16, 128 * jt:128 * (jt + 1)], idt_bf),
                   inc=(jt == 1), deps=(bdeps if jt == 0 else []))
            dv(lambda e: e.tensor_copy(b_tm, ptr[:, 0:256].rearrange("p (j n) -> p j n", j=2)),
               deps=[D(s_pe)])
            bank_rel[7] = D(s_dv)

            for jt in range(2):
                dv(lambda e, jt=jt: e.tensor_tensor(
                    xt_tm[:, jt, :].rearrange("p (h q) -> p h q", h=NH),
                    xssm_tm[:, jt, :].rearrange("p (h q) -> p h q", h=NH),
                    dt_tm[:, jt, :].unsqueeze(2).broadcast_to([128, NH, HD]), ALU.mult),
                   deps=([D(s_pe), D(s_ac)] if jt == 0 else []))
            for jt in range(2):
                dv(lambda e, jt=jt: e.tensor_tensor(
                    xtds_tm[:, jt, :].rearrange("p (h q) -> p h q", h=NH),
                    xt_tm[:, jt, :].rearrange("p (h q) -> p h q", h=NH),
                    ds_tm[:, jt, :].unsqueeze(2).broadcast_to([128, NH, HD]), ALU.mult))
            xt_dv = s_dv.n

            if not seq_start:
                for it in range(2):
                    for ncq in range(4):
                        bi = 2 + (it * 4 + ncq) % 2
                        pe(lambda e, it=it, ncq=ncq, bi=bi: e.matmul(
                            pb[bi], conv_s[:, 17, 128 * it:128 * (it + 1)],
                            s_bfb[:, 512 * ncq:512 * (ncq + 1)], start=True, stop=True),
                           inc=True,
                           deps=([(s_ac, conv_ac), (s_dv, prev_scan["dv"])] + brel(bi)))
                        dv(lambda e, it=it, ncq=ncq, bi=bi: e.tensor_tensor(
                            y_sb[:, it, 512 * ncq:512 * (ncq + 1)]
                                .rearrange("p (h q) -> p h q", h=8),
                            pb[bi].rearrange("p (h q) -> p h q", h=8),
                            ea_tm[:, it, 8 * ncq:8 * (ncq + 1)].unsqueeze(2)
                                .broadcast_to([128, 8, HD]), ALU.mult),
                           deps=[D(s_pe), D(s_ac)])
                        bank_rel[bi] = D(s_dv)
                yoff_pe = s_pe.n
                for it in range(2):
                    dv(lambda e, it=it: e.tensor_tensor(
                        tmp2k.rearrange("p (h q) -> p h q", h=NH),
                        xssm_tm[:, it, :].rearrange("p (h q) -> p h q", h=NH),
                        d_bc.unsqueeze(2).broadcast_to([128, NH, HD]), ALU.mult),
                       deps=([D(s_ac)] if it == 0 else []))
                    dv(lambda e, it=it: e.tensor_tensor(
                        y_sb[:, it, :], y_sb[:, it, :], tmp2k, ALU.add))
            else:
                yoff_pe = s_pe.n
                for it in range(2):
                    dv(lambda e, it=it: e.tensor_tensor(
                        y_sb[:, it, :].rearrange("p (h q) -> p h q", h=NH),
                        xssm_tm[:, it, :].rearrange("p (h q) -> p h q", h=NH),
                        d_bc.unsqueeze(2).broadcast_to([128, NH, HD]), ALU.mult),
                       deps=([D(s_ac)] if it == 0 else []))
            S["yinit_dv"] = s_dv.n

            st_banks = []
            for ncq in range(4):
                bi = ncq
                for jt in range(2):
                    pe(lambda e, ncq=ncq, jt=jt, bi=bi: e.matmul(
                        pb[bi], b_tm[:, jt, :], xtds_tm[:, jt, 512 * ncq:512 * (ncq + 1)],
                        start=(jt == 0), stop=(jt == 1)),
                       inc=(jt == 1),
                       deps=(([(s_dv, xt_dv)] + brel(bi)) if jt == 0 else []))
                st_banks.append((bi, s_pe.n))
            if not seq_start:
                dv(lambda e: e.tensor_tensor(
                    s_f32.rearrange("p (h q) -> p h q", h=NH),
                    s_f32.rearrange("p (h q) -> p h q", h=NH),
                    cd_bc.unsqueeze(2).broadcast_to([128, NH, HD]), ALU.mult),
                   deps=[D(s_ac), (s_pe, yoff_pe)])
                for ncq in range(4):
                    bi, pen = st_banks[ncq]
                    dv(lambda e, ncq=ncq, bi=bi: e.tensor_tensor(
                        s_f32[:, 512 * ncq:512 * (ncq + 1)],
                        s_f32[:, 512 * ncq:512 * (ncq + 1)], pb[bi], ALU.add),
                       deps=[(s_pe, pen)])
                    bank_rel[bi] = D(s_dv)
            else:
                for ncq in range(4):
                    bi, pen = st_banks[ncq]
                    dv(lambda e, ncq=ncq, bi=bi: e.tensor_copy(
                        s_f32[:, 512 * ncq:512 * (ncq + 1)], pb[bi]),
                       deps=[(s_pe, pen)])
                    bank_rel[bi] = D(s_dv)
            dv(lambda e: e.tensor_copy(s_bfb, s_f32), deps=[(s_pe, yoff_pe)])
            prev_scan["dv"] = s_dv.n
            dbg_tap("state%d" % c, s_f32, [128, NH * HD])

            for jt in range(2):
                pe(lambda e, jt=jt: e.matmul(
                    pb[6][:, 256 * jt:256 * (jt + 1)],
                    conv_s[:, 16, 128 * jt:128 * (jt + 1)], conv_s[:, 17, :],
                    start=True, stop=True),
                   inc=(jt == 1), deps=(([(s_ac, conv_ac)] + brel(6)) if jt == 0 else []))
            dv(lambda e: e.tensor_copy(cbt_bf, pb[6].rearrange("p (j l) -> p j l", j=2)),
               deps=[D(s_pe)])
            bank_rel[6] = D(s_dv)
            S["cbt_dv"] = s_dv.n
            S["cs_free"] = D(s_pe)

        def b_sloop(c):
            S = cst[c]
            ascr_ld = S["ascr_ld"]
            cbt_dv = S["cbt_dv"]
            yinit_dv = S["yinit_dv"]
            af_fill = {}
            af_rel = b_sloop.af_rel

            def load_aflat(pair):
                slot = pair % 3
                deps = [(s_as, ascr_ld)] + ([af_rel[slot]] if af_rel[slot] else [])
                dma("sync", af_bf[slot].rearrange("o (s n) -> o s n", s=2),
                    a_scr_h[c % 2][:, 512 * pair:512 * (pair + 1)].unsqueeze(0),
                    s_af[slot], deps=deps)
                af_fill[pair] = (slot, s_af[slot].n)

            for _p in range(3):
                load_aflat(_p)

            def ydbank(it, grp):
                return (2 + it) if grp % 2 == 0 else (0 + it)

            for pair in range(NH // 2):
                aslot = pair % 3
                afs, afv = af_fill[pair]
                for jt in range(2):
                    bank = 4 + jt
                    bdep = [(s_af[afs], afv)] + brel(bank)
                    if pair == 0:
                        bdep += [(s_in, init_in)]
                    pe(lambda e, aslot=aslot, bank=bank: e.matmul(
                        pb[bank], ones_bf, af_bf[aslot][0:1, 0:512],
                        start=True, stop=False),
                       deps=bdep)
                    pe(lambda e, aslot=aslot, bank=bank: e.matmul(
                        pb[bank], ones_bf, af_bf[aslot][0:1, 512:1024],
                        start=False, stop=False))
                    pe(lambda e, jt=jt, bank=bank: e.matmul(
                        pb[bank], idt_bf, maskdup[:, jt, :],
                        start=False, stop=True), inc=True)
                ai_pe = s_pe.n
                for hh in range(2):
                    h = 2 * pair + hh
                    sbuf = s_t[h % 2]
                    for jt in range(2):
                        acdeps = [(s_pe, ai_pe)]
                        if jt == 0 and st_rel[h % 2]:
                            acdeps.append(st_rel[h % 2])
                        ac(lambda e, hh=hh, jt=jt, h=h, sbuf=sbuf: e.activation(
                            sbuf[:, jt, :], pb[4 + jt][:, 256 * hh:256 * (hh + 1)],
                            AF.Exp, bias=a_tm_neg[:, jt, h:h + 1]),
                           deps=acdeps)
                        dv(lambda e, jt=jt, sbuf=sbuf: e.tensor_tensor(
                            sbuf[:, jt, :], sbuf[:, jt, :], cbt_bf[:, jt, :], ALU.mult),
                           deps=[D(s_ac), (s_dv, cbt_dv)])
                    s_dv_h = s_dv.n
                    grp = h // 8
                    for it in range(2):
                        bi = ydbank(it, grp)
                        col = (h % 8) * HD
                        first = (h % 8 == 0)
                        for jt in range(2):
                            pe(lambda e, it=it, jt=jt, h=h, bi=bi, col=col, sbuf=sbuf: e.matmul(
                                pb[bi][:, col:col + HD],
                                sbuf[:, jt, 128 * it:128 * (it + 1)],
                                xt_tm[:, jt, h * HD:(h + 1) * HD],
                                start=(jt == 0), stop=(jt == 1)),
                               inc=(jt == 1),
                               deps=(([(s_dv, s_dv_h)] + (brel(bi) if first else []))
                                     if jt == 0 else []))
                    st_rel[h % 2] = D(s_pe)
                bank_rel[4] = D(s_ac)
                bank_rel[5] = D(s_ac)
                if pair + 3 < NH // 2:
                    af_rel[aslot] = D(s_pe)
                    load_aflat(pair + 3)
                if pair % 4 == 3:
                    grp = pair // 4
                    for it in range(2):
                        bi = ydbank(it, grp)
                        dv(lambda e, it=it, grp=grp, bi=bi: e.scalar_tensor_tensor(
                            y_sb[:, it, 512 * grp:512 * (grp + 1)], pb[bi], 1.0,
                            y_sb[:, it, 512 * grp:512 * (grp + 1)], ALU.mult, ALU.add),
                           deps=[D(s_pe), (s_dv, yinit_dv)])
                        bank_rel[bi] = D(s_dv)
            dbg_tap("yssd%d" % c, y_sb, [128, 2, DI])
        b_sloop.af_rel = [None, None, None]

        def b_gate(c):
            S = cst[c]
            for it in range(2):
                dv(lambda e, it=it: e.tensor_tensor(
                    y_sb[:, it, :], y_sb[:, it, :], z_silu[:, it, :], ALU.mult),
                   deps=[D(s_ac)])
                accum_stat(8 + it, y_sb[:, it, :], AF.Square, tmp2k, deps=[D(s_dv)])
                ac(lambda e, it=it: e.activation(stat[:, 8 + it:9 + it],
                                                 stat[:, 8 + it:9 + it], AF.Ln,
                                                 scale=1.0 / DI, bias=stat[:, 15:16]))
                ac(lambda e, it=it: e.activation(stat[:, 8 + it:9 + it],
                                                 stat[:, 8 + it:9 + it], AF.Exp, scale=-0.5))
                dv(lambda e, it=it: e.tensor_scalar(tmp2k, y_sb[:, it, :],
                                                    stat[:, 8 + it:9 + it], None, ALU.mult),
                   deps=[D(s_ac)])
                dv(lambda e, it=it: e.tensor_tensor(yn_bf[:, it, :], tmp2k, normw_bc, ALU.mult))
            S["gate_dv"] = s_dv.n
            dbg_tap("ygate%d" % c, y_sb, [128, 2, DI])

        def b_yntr(c):
            S = cst[c]
            for b4 in range(4):
                transpose8(
                    [yn_bf[:, i % 2, 128 * (4 * b4 + i // 2):128 * (4 * b4 + i // 2 + 1)]
                     for i in range(8)],
                    y_fm[:, 4 * b4:4 * (b4 + 1), :].rearrange("p a (m t) -> p a m t", m=2),
                    [(s_dv, S["gate_dv"])])
            S["yfm_dv"] = s_ac.n

        def b_outproj(c, ncq):
            S = cst[c]
            cl = c % BLK
            xs = c % 2
            slot, fill = wout_tag[(c, ncq)]
            bi = 2 + ncq % 2
            dep0 = [(s_wo[slot], fill), (s_ac, S["yfm_dv"]), (s_xt, S["x_ld"])] + brel(bi)
            for mt in range(2):
                for kt in range(KI):
                    pe(lambda e, mt=mt, kt=kt, slot=slot, bi=bi: e.matmul(
                        pb[bi][:, 128 * mt:128 * (mt + 1)],
                        y_fm[:, kt, 128 * mt:128 * (mt + 1)],
                        wout_ring[slot][:, kt, :],
                        start=(kt == 0), stop=(kt == KI - 1)),
                       inc=(mt == 1 and kt == KI - 1),
                       deps=(dep0 if (mt == 0 and kt == 0) else []))
            wout_rel[slot] = D(s_pe)
            if ncq + 2 < WOUT_NS:
                load_wout(c, ncq + 2)
            dv(lambda e, ncq=ncq, bi=bi, cl=cl, xs=xs: e.scalar_tensor_tensor(
                x2_blk[:, 2 * cl:2 * cl + 2, 128 * ncq:128 * (ncq + 1)],
                pb[bi][:, 0:256].rearrange("p (m t) -> p m t", m=2), 1.0,
                x_tm[xs][:, :, 128 * ncq:128 * (ncq + 1)], ALU.mult, ALU.add),
               deps=[D(s_pe)])
            bank_rel[bi] = D(s_dv)
            if ncq == WOUT_NS - 1:
                S["r1_dv"] = D(s_dv)
                S["r1n"] = s_dv.n
                dbg_tap("r1_%d" % c, x2_blk, [128, 2 * BLK, DM])

        def b_ln1(c):
            S = cst[c]
            cl = c % BLK
            for mt in range(2):
                tt = 2 * cl + mt
                accum_stat(8, x2_blk[:, tt, :], AF.Square, sq_scr,
                           deps=[(s_dv, S["r1n"])])
                accum_stat(10, x2_blk[:, tt, :], AF.Copy, sq_scr)
                dv(lambda e: e.tensor_scalar_mul(stat[:, 10:11], stat[:, 10:11], 1.0 / DM),
                   deps=[D(s_ac)])
                dv(lambda e: e.tensor_tensor(stat[:, 12:13], stat[:, 10:11],
                                             stat[:, 10:11], ALU.mult), deps=[D(s_dv)])
                dv(lambda e: e.scalar_tensor_tensor(stat[:, 8:9], stat[:, 8:9], 1.0 / DM,
                                                    stat[:, 12:13], ALU.mult, ALU.subtract),
                   deps=[D(s_dv)])
                ac(lambda e: e.activation(stat[:, 8:9], stat[:, 8:9], AF.Ln,
                                          bias=stat[:, 15:16]), deps=[D(s_dv)])
                ac(lambda e: e.activation(stat[:, 8:9], stat[:, 8:9], AF.Exp, scale=-0.5))
                dv(lambda e, tt=tt: e.tensor_scalar(
                    x2_blk[:, tt, :], x2_blk[:, tt, :], stat[:, 10:11],
                    stat[:, 8:9], ALU.subtract, ALU.mult),
                   deps=[D(s_ac)])
            dbg_tap("x2_%d" % c, x2_blk, [128, 2 * BLK, DM])

        def b_x2router(c):
            cl = c % BLK
            dv(lambda e, cl=cl: e.tensor_copy(x2bf, x2_blk[:, 2 * cl:2 * cl + 2, :]),
               deps=[D(s_pe)])
            x2bf_dv = s_dv.n
            for g in range(2):
                transpose8(
                    [x2bf[:, i % 2, 128 * (g * 4 + i // 2):128 * (g * 4 + i // 2 + 1)]
                     for i in range(8)],
                    x2_fm[:, 4 * g:4 * (g + 1), cl * L:(cl + 1) * L]
                        .rearrange("p a (m t) -> p a m t", m=2),
                    [(s_dv, x2bf_dv)])
            x2fm_dv = s_ac.n

            for mt in range(2):
                for kt in range(KD):
                    pe(lambda e, mt=mt, kt=kt, cl=cl: e.matmul(
                        pb[6][:, 16 * mt:16 * mt + NE],
                        x2_fm[:, kt, cl * L + 128 * mt:cl * L + 128 * (mt + 1)],
                        wr_bf[:, kt, :], start=(kt == 0), stop=(kt == KD - 1)),
                       inc=(mt == 1 and kt == KD - 1),
                       deps=(([(s_ac, x2fm_dv), (s_in, init_in)] + brel(6))
                             if (mt == 0 and kt == 0) else []))
            rt_pe = s_pe.n
            for mt in range(2):
                tt = 2 * cl + mt
                dv(lambda e, mt=mt: e.reduce_max(stat[:, 13:14],
                                                 pb[6][:, 16 * mt:16 * mt + NE], axis=AX.X),
                   deps=[(s_pe, rt_pe)])
                dv(lambda e, mt=mt: e.tensor_scalar(probs_p[:, 0:NE],
                                                    pb[6][:, 16 * mt:16 * mt + NE],
                                                    stat[:, 13:14], None, ALU.subtract),
                   deps=[D(s_dv)])
                ac(lambda e: e.activation(probs_p[:, 0:NE], probs_p[:, 0:NE], AF.Exp),
                   deps=[D(s_dv)])
                dv(lambda e: e.max(out=m8_p, in_=probs_p), deps=[D(s_ac)])
                dv(lambda e: e.memset(m8_p[:, 2:8], -2.0), deps=[D(s_dv)])
                dv(lambda e: e.match_replace(out=zap_p, in_to_replace=m8_p,
                                             in_values=probs_p, imm_value=0.0),
                   deps=[D(s_dv)])
                dv(lambda e, tt=tt: e.tensor_tensor(cw_blk[:, tt, :], probs_p, zap_p,
                                                    ALU.subtract), deps=[D(s_dv)])
                dv(lambda e: e.tensor_tensor(stat[:, 13:14], m8_p[:, 0:1], m8_p[:, 1:2],
                                             ALU.add), deps=[D(s_dv)])
                dv(lambda e: e.reciprocal(stat[:, 13:14], stat[:, 13:14]), deps=[D(s_dv)])
                dv(lambda e, tt=tt: e.tensor_scalar(cw_blk[:, tt, :], cw_blk[:, tt, :],
                                                    stat[:, 13:14], None, ALU.mult),
                   deps=[D(s_dv)])
            bank_rel[6] = D(s_dv)
            dbg_tap("cw%d" % c, cw_blk, [128, 2 * BLK, 16])

        wgu_rel = {"g": [None] * 3, "u": [None] * 3}
        wd_rel = [None, None]
        slab_ctr = {"g": 0, "u": 0}
        WG_NS = 11
        gs_rel = [None, None]
        hid_rel = [None]

        def moe_block(blk, phase_deps):
            def load_wgu(which, e_, s, extra=()):
                ring = wg_flat if which == "g" else wu_flat
                sems = s_wg if which == "g" else s_wu
                src = wg_h if which == "g" else wu_h
                slot = slab_ctr[which] % 3
                slab_ctr[which] += 1
                deps = list(extra) + ([wgu_rel[which][slot]] if wgu_rel[which][slot] else [])
                dma("gpsimd", ring[slot],
                    src[e_][:, s * 2048:(s + 1) * 2048], sems[slot], deps=deps)
                return (slot, sems[slot].n)

            wd_tag = {}
            WDP = KF * DM // 8

            def load_wd_piece(e_, k, extra=()):
                slot = e_ % 2
                deps = list(extra) + ([wd_rel[slot]] if wd_rel[slot] else [])
                dma("gpsimd", wd_flat[slot][:, WDP * k:WDP * (k + 1)],
                    wd_h[e_][:, WDP * k:WDP * (k + 1)], s_wd[slot], deps=deps)
                wd_tag[e_] = (slot, s_wd[slot].n)

            pre_g = {}
            pre_u = {}
            for s in range(3):
                pre_g[s] = load_wgu("g", 0, s,
                                    extra=(phase_deps if s == 0 else ()))
                pre_u[s] = load_wgu("u", 0, s)
            for k in range(8):
                load_wd_piece(0, k, extra=(phase_deps if k == 0 else ()))
            for e_ in range(NE):
                wslot, wd_fill = wd_tag[e_]
                for g in range(2):
                    gt = slice(512 * g, 512 * (g + 1))
                    if e_ == 0 and g == 0:
                        tags_g = dict(pre_g)
                        tags_u = dict(pre_u)
                    else:
                        tags_g = {}
                        tags_u = {}
                        for s in range(3):
                            tags_g[s] = load_wgu("g", e_, s)
                            tags_u[s] = load_wgu("u", e_, s)
                    for m in range(KF):
                        s = m // 2
                        mi = m % 2
                        gslot, gfill = tags_g[s]
                        uslot, ufill = tags_u[s]
                        wid0 = 128 * mi
                        gb = m % 2
                        ub = 2 + m % 2
                        gdep = [(s_wg[gslot], gfill)] + brel(gb)
                        if m == 0:
                            gdep += [D(s_dv)]
                        for kt in range(KD):
                            pe(lambda e, kt=kt, gslot=gslot, wid0=wid0, gb=gb, gt=gt: e.matmul(
                                pb[gb], wg_ring[gslot][:, kt, wid0:wid0 + 128],
                                x2_fm[:, kt, gt], start=(kt == 0), stop=(kt == KD - 1)),
                               inc=(kt == KD - 1), deps=(gdep if kt == 0 else []))
                        last_of_slab = (mi == 1)
                        if last_of_slab:
                            wgu_rel["g"][gslot] = D(s_pe)
                            if s + 3 < WG_NS:
                                tags_g[s + 3] = load_wgu("g", e_, s + 3)
                            if g == 0 and e_ + 1 < NE and s < 8:
                                load_wd_piece(e_ + 1, s)
                        acdep = [D(s_pe)] + ([gs_rel[m % 2]] if gs_rel[m % 2] else [])
                        ac(lambda e, m=m, gb=gb: e.activation(gate_s[m % 2], pb[gb], AF.Silu),
                           deps=acdep)
                        bank_rel[gb] = D(s_ac)
                        udep = [(s_wu[uslot], ufill)] + brel(ub)
                        for kt in range(KD):
                            pe(lambda e, kt=kt, uslot=uslot, wid0=wid0, ub=ub, gt=gt: e.matmul(
                                pb[ub], wu_ring[uslot][:, kt, wid0:wid0 + 128],
                                x2_fm[:, kt, gt], start=(kt == 0), stop=(kt == KD - 1)),
                               inc=(kt == KD - 1), deps=(udep if kt == 0 else []))
                        if last_of_slab:
                            wgu_rel["u"][uslot] = D(s_pe)
                            if s + 3 < WG_NS:
                                tags_u[s + 3] = load_wgu("u", e_, s + 3)
                        hdep = [D(s_pe), D(s_ac)] + \
                            ([hid_rel[0]] if (hid_rel[0] and m == 0) else [])
                        dv(lambda e, m=m, ub=ub: e.tensor_tensor(
                            hid_bf[:, m, :], pb[ub], gate_s[m % 2], ALU.mult), deps=hdep)
                        gs_rel[m % 2] = D(s_dv)
                        bank_rel[ub] = D(s_dv)
                    hid_dv = s_dv.n
                    for tq in range(4):
                        tt = 4 * g + tq
                        for nch2 in range(2):
                            bi = 4 + (tq * 2 + nch2) % 3
                            dep0 = [(s_dv, hid_dv), (s_wd[wslot], wd_fill)] + brel(bi)
                            for kt in range(KF):
                                pe(lambda e, tq=tq, nch2=nch2, kt=kt, bi=bi,
                                   wslot=wslot: e.matmul(
                                    pb[bi], hid_bf[:, kt, 128 * tq:128 * (tq + 1)],
                                    wd_buf[wslot][:, kt, 512 * nch2:512 * (nch2 + 1)],
                                    start=(kt == 0), stop=(kt == KF - 1)),
                                   inc=(kt == KF - 1), deps=(dep0 if kt == 0 else []))
                            dv(lambda e, tt=tt, nch2=nch2, bi=bi, e_=e_: e.scalar_tensor_tensor(
                                x2_blk[:, tt, 512 * nch2:512 * (nch2 + 1)], pb[bi],
                                cw_blk[:, tt, e_:e_ + 1],
                                x2_blk[:, tt, 512 * nch2:512 * (nch2 + 1)],
                                ALU.mult, ALU.add),
                               deps=[D(s_pe)])
                            bank_rel[bi] = D(s_dv)
                    hid_rel[0] = D(s_pe)
                wd_rel[wslot] = D(s_pe)

            dbg_tap("moe%d" % blk, x2_blk, [128, 2 * BLK, DM])

            moe_dv = s_dv.n
            for tq in range(2 * BLK):
                accum_stat(8, x2_blk[:, tq, :], AF.Square, dump_m,
                           deps=[(s_dv, moe_dv)])
                accum_stat(10, x2_blk[:, tq, :], AF.Copy, dump_m)
                dv(lambda e: e.tensor_scalar_mul(stat[:, 10:11], stat[:, 10:11], 1.0 / DM),
                   deps=[D(s_ac)])
                dv(lambda e: e.tensor_tensor(stat[:, 12:13], stat[:, 10:11],
                                             stat[:, 10:11], ALU.mult), deps=[D(s_dv)])
                dv(lambda e: e.scalar_tensor_tensor(stat[:, 8:9], stat[:, 8:9], 1.0 / DM,
                                                    stat[:, 12:13], ALU.mult, ALU.subtract),
                   deps=[D(s_dv)])
                ac(lambda e: e.activation(stat[:, 8:9], stat[:, 8:9], AF.Ln,
                                          bias=stat[:, 15:16]), deps=[D(s_dv)])
                ac(lambda e: e.activation(stat[:, 8:9], stat[:, 8:9], AF.Exp, scale=-0.5))
                dv(lambda e, tq=tq: e.tensor_scalar(
                    x2_blk[:, tq, :], x2_blk[:, tq, :], stat[:, 10:11], stat[:, 8:9],
                    ALU.subtract, ALU.mult), deps=[D(s_ac)])
                r0 = blk * BT + tq * 128
                dma("sync", out_h[r0:r0 + 128, :], x2_blk[:, tq, :], s_stt[tq % 2],
                    deps=[D(s_dv)])
                store_tags.append((s_stt[tq % 2], s_stt[tq % 2].n))

        store_tags = []
        pd = fence()
        for blk in range((n_chunks + BLK - 1) // BLK):
            c0 = blk * BLK
            a_dma(c0, pd)
            a_xfm(c0)
            for sl in SLAB_ORDER:
                a_inproj_slab(c0, sl)
            a_dt(c0)
            a_conv(c0)
            for cl in range(BLK):
                c = c0 + cl
                nxt = c + 1 if (cl + 1 < BLK and c + 1 < n_chunks) else None
                if nxt is not None:
                    a_dma(nxt)
                b_head(c)
                b_sloop(c)
                if nxt is not None:
                    a_xfm(nxt)
                b_gate(c)
                if nxt is not None:
                    for k in range(3):
                        a_inproj_slab(nxt, SLAB_ORDER[k])
                b_yntr(c)
                for k in range(3, 11):
                    if k - 3 < WOUT_NS:
                        b_outproj(c, k - 3)
                    if nxt is not None:
                        a_inproj_slab(nxt, SLAB_ORDER[k])
                if nxt is not None:
                    a_dt(nxt)
                    a_conv(nxt)
                    for k in range(11, 14):
                        a_inproj_slab(nxt, SLAB_ORDER[k])
                b_ln1(c)
                if nxt is not None:
                    for k in range(14, WIN_NS):
                        a_inproj_slab(nxt, SLAB_ORDER[k])
                b_x2router(c)
                if c >= n_chunks - 1:
                    break
            pd = fence()
            if do_moe:
                moe_block(blk, pd)
                pd = fence() + [D(s_stt[0]), D(s_stt[1])]

        P.wait("sync", s_sp, s_sp.n)
        P.wait("sync", s_stt[0], s_stt[0].n)
        P.wait("sync", s_stt[1], s_stt[1].n)
        P.wait("sync", s_xt, s_xt.n)
        P.wait("vector", s_stt[0], s_stt[0].n)
        P.wait("vector", s_stt[1], s_stt[1].n)

        blkctx = st.enter_context(nc.Block())
        P.emit(blkctx)

    return nc, dbg_h


def _consts():
    idt_bf = np.eye(128, dtype=ml_dtypes.bfloat16)
    idt_f = np.eye(128, dtype=np.float32)
    ones_r = np.ones((1, 128), dtype=np.float32)
    mask = np.zeros((2, 128, 512), dtype=np.float32)
    for jt in range(2):
        for j in range(128):
            jg = jt * 128 + j
            m = np.where(np.arange(L) >= jg, 0.0, -MASKVAL).astype(np.float32)
            mask[jt, j, 0:L] = m
            mask[jt, j, L:2 * L] = m
    return idt_bf, idt_f, ones_r, mask


def _slabify(W, widths):
    K, N = W.shape
    kd = K // 128
    Wr = np.asarray(W, np.float32).reshape(kd, 128, N).transpose(1, 0, 2)
    parts, col = [], 0
    for w in widths:
        parts.append(Wr[:, :, col:col + w].reshape(128, kd * w))
        col += w
    return np.ascontiguousarray(np.concatenate(parts, axis=1))


def make_in_maps(inputs, ncores=8):
    x = np.ascontiguousarray(inputs["L0_attn_hidden_states"], dtype=np.float32)
    per = x.shape[0] // ncores
    idt_bf, idt_f, ones_r, mask = _consts()
    wg_f = np.asarray(inputs["L0_ffn_Wg"], np.float32)
    wu_f = np.asarray(inputs["L0_ffn_Wu"], np.float32)
    wd_f = np.asarray(inputs["L0_ffn_Wd"], np.float32)
    wgu_widths = [256] * 11
    common = dict(
        w_in=_slabify(np.asarray(inputs["L0_attn_W_in"], np.float32),
                      [256] * 17 + [32]),
        conv_w=np.ascontiguousarray(inputs["L0_attn_conv_w"], np.float32),
        conv_b=np.ascontiguousarray(inputs["L0_attn_conv_b"], np.float32),
        dt_bias=np.ascontiguousarray(inputs["L0_attn_dt_bias"], np.float32),
        a_log=np.ascontiguousarray(inputs["L0_attn_A_log"], np.float32),
        dd=np.ascontiguousarray(inputs["L0_attn_D"], np.float32),
        norm_w=np.ascontiguousarray(inputs["L0_attn_norm_w"], np.float32),
        w_out=_slabify(np.asarray(inputs["L0_attn_W_out"], np.float32),
                       [128] * 8),
        wr=np.ascontiguousarray(inputs["L0_ffn_Wr"], np.float32),
        wg=np.stack([_slabify(wg_f[e], wgu_widths) for e in range(NE)]),
        wu=np.stack([_slabify(wu_f[e], wgu_widths) for e in range(NE)]),
        wd=np.stack([_slabify(wd_f[e], [DM]) for e in range(NE)]),
        idt_bf=idt_bf, idt_f=idt_f, ones_r=ones_r, maskbig=mask,
    )
    in_maps = []
    for ci in range(ncores):
        m = dict(common)
        m["x"] = np.ascontiguousarray(x[ci * per:(ci + 1) * per].reshape(T, DM))
        in_maps.append(m)
    return in_maps, per


def kernel(L0_attn_hidden_states, L0_attn_W_in, L0_attn_conv_w, L0_attn_conv_b,
           L0_attn_dt_bias, L0_attn_A_log, L0_attn_D, L0_attn_norm_w, L0_attn_W_out,
           L0_ffn_Wr, L0_ffn_Wg, L0_ffn_Wu, L0_ffn_Wd):
    inputs = dict(
        L0_attn_hidden_states=L0_attn_hidden_states, L0_attn_W_in=L0_attn_W_in,
        L0_attn_conv_w=L0_attn_conv_w, L0_attn_conv_b=L0_attn_conv_b,
        L0_attn_dt_bias=L0_attn_dt_bias, L0_attn_A_log=L0_attn_A_log,
        L0_attn_D=L0_attn_D, L0_attn_norm_w=L0_attn_norm_w,
        L0_attn_W_out=L0_attn_W_out, L0_ffn_Wr=L0_ffn_Wr, L0_ffn_Wg=L0_ffn_Wg,
        L0_ffn_Wu=L0_ffn_Wu, L0_ffn_Wd=L0_ffn_Wd)
    in_maps, per = make_in_maps(inputs)
    nc, _ = build()
    res = run_bass_kernel_spmd(nc, in_maps, core_ids=list(range(8)))
    outs = [r["out"].reshape(per, T // per, DM) for r in res.results]
    return np.concatenate(outs, axis=0)
